# version 20
# baseline (speedup 1.0000x reference)
import numpy as np
import ml_dtypes
import time
from contextlib import ExitStack, contextmanager

import concourse.bass as bass
import concourse.mybir as mybir
from concourse.bass_utils import run_bass_kernel_spmd
from concourse.expressions import RuntimeValue

F32 = mybir.dt.float32
BF16 = mybir.dt.bfloat16
I32 = mybir.dt.int32
AF = mybir.ActivationFunctionType
ALU = mybir.AluOpType
AX = mybir.AxisListType
NBF = ml_dtypes.bfloat16
ds = bass.ds

D = 1024
LSEQ = 8192
T = 4096
TWO_PI = 6.283185307179586
MAGICF = float(np.float32(1.5 * 2 ** 23))


class Pr:

    def __init__(self, nc, ctx):
        self.nc = nc
        self.ctx = ctx
        self.C = ctx.enter_context(nc.semaphore("C"))
        self.Dm = ctx.enter_context(nc.semaphore("Dm"))
        self.c = 0
        self.d = 0
        self.nn = 0

    @contextmanager
    def scope(self):
        old = self.ctx
        with ExitStack() as es:
            self.ctx = es
            try:
                yield
            finally:
                self.ctx = old

    def sb(self, name, shape, dt=F32):
        self.nn += 1
        return self.ctx.enter_context(self.nc.sbuf_tensor(f"{name}_{self.nn}", shape, dt))

    def ps(self, name, shape, dt=F32):
        self.nn += 1
        return self.ctx.enter_context(self.nc.psum_tensor(f"{name}_{self.nn}", shape, dt))

    def _w(self, eng):
        eng.wait_ge(self.C, self.c)
        eng.wait_ge(self.Dm, 16 * self.d)

    def op(self, eng, fn, *a, **k):
        self._w(eng)
        ins = fn(*a, **k)
        self.c += 1
        ins.then_inc(self.C, 1)
        return ins

    def mm(self, *a, **k):
        t = self.nc.tensor
        self._w(t)
        ins = t.matmul(*a, **k)
        self.c += 1
        ins.then_inc(self.C, 1)
        return ins

    def mmgrp(self, calls):
        t = self.nc.tensor
        self._w(t)
        n = len(calls)
        for i, (o, l, r) in enumerate(calls):
            ins = t.matmul(o, l, r, start=(i == 0), stop=(i == n - 1))
        self.c += 1
        ins.then_inc(self.C, 1)

    def tr(self, out_psum, in_sbuf, ident):
        t = self.nc.tensor
        self._w(t)
        ins = t.transpose(out_psum, in_sbuf, ident)
        self.c += 1
        ins.then_inc(self.C, 1)

    def dma(self, eng, out, in_, **k):
        k.pop("bulk", None)
        self._w(eng)
        if isinstance(out, tuple) or isinstance(in_, tuple):
            o0, o1 = out if isinstance(out, tuple) else (out, out)
            i0, i1 = in_ if isinstance(in_, tuple) else (in_, in_)
            with eng.If_eq(self.slot_reg, 0):
                eng.dma_start(out=o0, in_=i0, **k).then_inc(self.Dm, 16)
            with eng.Else():
                ins = eng.dma_start(out=o1, in_=i1, **k).then_inc(self.Dm, 16)
            self.d += 1
            return ins
        ins = eng.dma_start(out=out, in_=in_, **k)
        self.d += 1
        ins.then_inc(self.Dm, 16)
        return ins

    def fin(self, eng):
        eng.wait_ge(self.C, self.c)
        eng.wait_ge(self.Dm, 16 * self.d)

    def set_fence(self, eng):
        ins = self.nc.gpsimd.engine_nop()
        self.c += 1
        ins.then_inc(self.C, 1)

    def floor_fence(self):
        pass

    force_serial = False


class Pr2(Pr):

    def __init__(self, nc, ctx):
        self.nc = nc
        self.ctx = ctx
        self.c = 0
        self.d = 0
        self.nn = 0
        self.sems = {}
        lane_names = ["ds0", "ds1", "ds2", "ds3", "dg0", "dg1", "dg2", "dg3"]
        for name in ["pe", "dv", "sc", "gp"] + lane_names:
            self.sems[name] = ctx.enter_context(nc.semaphore("S" + name))
        self.cnt = {k: 0 for k in self.sems}
        self.lane_pool = {"sync": ["ds0", "ds1", "ds3"],
                          "gp": ["dg0", "dg1", "dg2"]}
        self.bulk_lane = {"sync": "ds2", "gp": "dg3"}
        self.lane_rr = {"sync": 0, "gp": 0}
        self.hw_rr = 0
        self.slot_regs = {}
        self.waited = {k: {} for k in ("pe", "dv", "sc", "gp", "sync")}
        self.acc = {}
        self.fence_tok = None
        self.floor = {}
        self.force_serial = False

    @staticmethod
    def _ename(eng):
        v = eng.engine.value if hasattr(eng, "engine") else ""
        return {"PE": "pe", "DVE": "dv", "Activation": "sc", "Pool": "gp",
                "SP": "sync"}.get(v, "sync")

    @staticmethod
    def _span(ap):
        off = ap.offset
        if not isinstance(off, int):
            return None
        hi = off
        for st, cnt_ in ap.ap:
            hi += st * (cnt_ - 1)
        return (off, hi + 1)

    def _apinfo(self, ap):
        t = ap.tensor
        name = t.name
        is_psum = "PSUM" in str(getattr(t, "memory_space", "")) or hasattr(t, "bank")
        try:
            from concourse.bass_types import PSumTensorHandle
            is_psum = isinstance(t, PSumTensorHandle)
        except Exception:
            pass
        sp = None if is_psum else self._span(ap)
        if sp is None:
            sp = (0, 1 << 60)
        return name, sp

    def _deps_for(self, reads, writes):
        if self.force_serial:
            return {s: v for s, v in self.cnt.items() if v > 0}
        deps = {}
        def add(tok):
            s, v = tok
            if deps.get(s, 0) < v:
                deps[s] = v
        for ap in reads:
            name, (lo, hi) = self._apinfo(ap)
            for kind, l2, h2, tok in self.acc.get(name, ()):
                if kind == "w" and l2 < hi and lo < h2:
                    add(tok)
        for ap in writes:
            name, (lo, hi) = self._apinfo(ap)
            for kind, l2, h2, tok in self.acc.get(name, ()):
                if l2 < hi and lo < h2:
                    add(tok)
        if self.fence_tok is not None:
            add(self.fence_tok)
        for s, v in self.floor.items():
            if deps.get(s, 0) < v:
                deps[s] = v
        return deps

    def _wait(self, eng, deps):
        en = self._ename(eng)
        wm = self.waited[en]
        for s, v in sorted(deps.items()):
            if en == "pe" and s == "pe":
                continue
            if wm.get(s, 0) >= v:
                continue
            eng.wait_ge(self.sems[s], v)
            wm[s] = v

    def _record(self, reads, writes, tok):
        for kind, aps in (("r", reads), ("w", writes)):
            for ap in aps:
                name, (lo, hi) = self._apinfo(ap)
                lst = self.acc.setdefault(name, [])
                lst.append((kind, lo, hi, tok))
                if len(lst) > 96:
                    old, new = lst[:48], lst[48:]
                    mx = {}
                    anyw = False
                    for k2, l2, h2, (s2, v2) in old:
                        anyw = anyw or (k2 == "w")
                        if mx.get(s2, 0) < v2:
                            mx[s2] = v2
                    merged = [("w" if anyw else "r", 0, 1 << 60, (s2, v2))
                              for s2, v2 in mx.items()]
                    self.acc[name] = merged + new

    def _bump(self, en, n=1):
        self.cnt[en] += n
        return (en, self.cnt[en])

    @staticmethod
    def _extract_aps(args, kwargs=()):
        out = []
        for a in args:
            if isinstance(a, bass.AP):
                out.append(a)
            elif isinstance(a, tuple):
                out.extend(x for x in a if isinstance(x, bass.AP))
        return out

    def op(self, eng, fn, *a, **k):
        aps = self._extract_aps(a) + self._extract_aps(list(k.values()))
        writes, reads = aps[:1], aps[1:]
        deps = self._deps_for(reads, writes)
        self._wait(eng, deps)
        ins = fn(*a, **k)
        en = self._ename(eng)
        tok = self._bump(en)
        ins.then_inc(self.sems[en], 1)
        self._record(reads, writes, tok)
        return ins

    def mm(self, out, lhsT, rhs, **k):
        t = self.nc.tensor
        reads, writes = [lhsT, rhs], [out]
        deps = self._deps_for(reads, writes)
        self._wait(t, deps)
        ins = t.matmul(out, lhsT, rhs, **k)
        tok = self._bump("pe")
        ins.then_inc(self.sems["pe"], 1)
        self._record(reads, writes, tok)
        return ins

    def mmgrp(self, calls):
        t = self.nc.tensor
        reads = [l for _, l, r in calls] + [r for _, l, r in calls]
        writes = [o for o, _, _ in calls]
        deps = self._deps_for(reads, writes)
        self._wait(t, deps)
        n = len(calls)
        for i, (o, l, r) in enumerate(calls):
            ins = t.matmul(o, l, r, start=(i == 0), stop=(i == n - 1))
        tok = self._bump("pe")
        ins.then_inc(self.sems["pe"], 1)
        self._record(reads, writes, tok)

    def tr(self, out_psum, in_sbuf, ident):
        t = self.nc.tensor
        reads, writes = [in_sbuf, ident], [out_psum]
        deps = self._deps_for(reads, writes)
        self._wait(t, deps)
        ins = t.transpose(out_psum, in_sbuf, ident)
        tok = self._bump("pe")
        ins.then_inc(self.sems["pe"], 1)
        self._record(reads, writes, tok)
        return ins

    def dma(self, eng, out, in_, **k):
        bulk = k.pop("bulk", False)
        outs0 = out[0] if isinstance(out, tuple) else out
        ins0 = in_[0] if isinstance(in_, tuple) else in_
        cast = getattr(outs0, "dtype", None) != getattr(ins0, "dtype", None)
        accum = "accum_op" in k
        if cast or accum:
            eng, pool = self.nc.gpsimd, "gp"
        else:
            eng, pool = self.nc.sync, "sync"
        en = self._ename(eng)
        if bulk:
            lane = self.bulk_lane[pool]
        else:
            lanes = self.lane_pool[pool]
            lane = lanes[self.lane_rr[pool] % len(lanes)]
            self.lane_rr[pool] += 1
        outs = list(out) if isinstance(out, tuple) else [out]
        ins_ = list(in_) if isinstance(in_, tuple) else [in_]
        reads = list(ins_)
        if "accum_op" in k:
            reads = reads + outs
        deps = self._deps_for(reads, outs)
        if self.cnt[lane] > 0:
            deps[lane] = max(deps.get(lane, 0), self.cnt[lane])
        self._wait(eng, deps)
        if isinstance(out, tuple) or isinstance(in_, tuple):
            o0, o1 = out if isinstance(out, tuple) else (out, out)
            i0, i1 = in_ if isinstance(in_, tuple) else (in_, in_)
            with eng.If_eq(self.slot_regs[en], 0):
                eng.dma_start(out=o0, in_=i0, **k).then_inc(self.sems[lane], 16)
            with eng.Else():
                inst = eng.dma_start(out=o1, in_=i1, **k).then_inc(self.sems[lane], 16)
        else:
            inst = eng.dma_start(out=out, in_=in_, **k)
            inst.then_inc(self.sems[lane], 16)
        tok = self._bump(lane, 16)
        self._record(reads, outs, tok)
        return inst

    def _w(self, eng):
        en = self._ename(eng)
        wm = self.waited[en]
        for s, v in self.cnt.items():
            if v > 0 and wm.get(s, 0) < v:
                eng.wait_ge(self.sems[s], v)
                wm[s] = v

    def set_fence(self, eng_nop_engine):
        GP = self.nc.gpsimd
        ins = GP.engine_nop()
        tok = self._bump("gp")
        ins.then_inc(self.sems["gp"], 1)
        self.fence_tok = tok

    def floor_fence(self):
        for s, v in self.cnt.items():
            if v > 0 and self.floor.get(s, 0) < v:
                self.floor[s] = v

    def fin(self, eng):
        self._w(eng)


def pipeline(mk_gen, n, width):
    act = []
    nxt = 0
    while act or nxt < n:
        while len(act) < width and nxt < n:
            act.append(mk_gen(nxt))
            nxt += 1
        for g in list(act):
            try:
                next(g)
            except StopIteration:
                act.remove(g)


def _consts(nc, p):
    cs = {}
    eye = np.eye(128, dtype=np.float32)
    cs["idf"] = nc.inline_tensor(eye, name="idf")
    cs["idb"] = nc.inline_tensor(eye.astype(NBF), name="idb")
    cs["msl"] = nc.inline_tensor(np.tril(np.ones((128, 128), np.float32), -1), name="msl")
    cs["mui"] = nc.inline_tensor(np.triu(np.ones((128, 128), np.float32)), name="mui")
    cs["onesf"] = nc.inline_tensor(np.ones((1, 128), np.float32), name="onesf")
    cs["onescol_f"] = nc.inline_tensor(np.ones((128, 1), np.float32), name="onescol_f")
    t = {}
    for k, v in cs.items():
        dt = BF16 if k == "idb" else F32
        sbuf = p.sb("c_" + k, list(v.shape), dt)
        p.dma(nc.sync, sbuf[:], v.ap())
        t[k] = sbuf
    return t



def emit_l1(nc, p, C, io):
    V, SC, GP, SY = nc.vector, nc.scalar, nc.gpsimd, nc.sync
    with p.scope():
        wq_s = p.sb("wq_s", [128, 8 * 512], BF16)
        wk_s = p.sb("wk_s", [128, 8 * 512], BF16)
        wv_s = p.sb("wv_s", [128, 8 * 512], BF16)
        for dst, src in ((wq_s, io["wq"]), (wk_s, io["wk"]), (wv_s, io["wv"])):
            p.dma(GP, dst[:].rearrange("p (c n) -> p c n", c=8),
                  src.rearrange("(c p) n -> p c n", p=128))
        wbg_s = p.sb("wbg_s", [128, 8 * 4], BF16)
        p.dma(GP, wbg_s[:].rearrange("p (c n) -> p c n", c=8),
              io["wbg"].rearrange("(c p) n -> p c n", p=128))
        wmx_s = p.sb("wmx_s", [128, 8 * 2], BF16)
        p.dma(GP, wmx_s[:].rearrange("p (c n) -> p c n", c=8),
              io["wmx"].rearrange("(c p) n -> p c n", p=128))
        cq_s = p.sb("cq_s", [128, 4 * 4])
        ck_s = p.sb("ck_s", [128, 4 * 4])
        cv_s = p.sb("cv_s", [128, 4 * 4])
        for dst, src in ((cq_s, io["cq"]), (ck_s, io["ck"]), (cv_s, io["cv"])):
            p.dma(SY, dst[:].rearrange("p (c n) -> p c n", c=4),
                  src.rearrange("(c p) n -> p c n", p=128))
        cmx_s = p.sb("cmx_s", [2, 4])
        p.dma(SY, cmx_s[:], io["cmx"])
        mb_s = p.sb("mb_s", [2, 1])
        p.dma(SY, mb_s[:], io["mbias"])
        nmb_s = p.sb("nmb_s", [2, 1])
        rw_s = p.sb("rw_s", [128, 2])
        p.dma(SY, rw_s[:], io["rmsw"])

        epsa = p.sb("epsa", [128, 2])
        xt = p.sb("xt", [128, D])
        xc = p.sb("xc", [128, D])
        sqb = p.sb("sqb", [128, D])
        stats = p.sb("stats", [128, 8])
        xlb = p.sb("xlb", [128, D], BF16)
        xfm = p.sb("xfm", [128, 8 * 512], BF16)
        qfm = p.sb("qfm", [128, 4 * 512], BF16)
        kfm = p.sb("kfm", [128, 4 * 512], BF16)
        vfm = p.sb("vfm", [128, 4 * 512], BF16)
        pad = p.sb("pad", [128, 515])
        cvt = p.sb("cvt", [128, 512])
        cvt2 = p.sb("cvt2", [128, 512])
        tmpf = p.sb("tmpf", [128, 512])
        carry = p.sb("carry", [128, 3 * 12])
        mpad = p.sb("mpad", [2, 515])
        mcar = p.sb("mcar", [2, 3])
        mixf = p.sb("mixf", [2, 512])
        mixe = p.sb("mixe", [2, 512])
        mixt = p.sb("mixt", [128, 2 * 4])
        bgt = p.sb("bgt", [128, 4 * 4])
        sgt = p.sb("sgt", [128, 8])
        nbgt = p.sb("nbgt", [128, 4 * 4])
        alp = p.sb("alp", [128, 8])
        kgm = p.sb("kgm", [128, 8])
        bkg = p.sb("bkg", [128, 8])
        sqf = p.sb("sqf", [128, 512])
        khat_p = [p.sb("khat", [128, 256], BF16) for _ in range(2)]
        bkh_p = [p.sb("bkh", [128, 256], BF16) for _ in range(2)]
        bvt_p = [p.sb("bvt", [128, 256], BF16) for _ in range(2)]
        khf_p = [p.sb("khf", [128, 256], BF16) for _ in range(2)]
        nsb_p = [p.sb("nsb", [128, 128], BF16) for _ in range(2)]
        ntb_p = [p.sb("ntb", [128, 128], BF16) for _ in range(2)]
        p2a_p = [p.sb("p2a", [128, 128], BF16) for _ in range(2)]
        p2b_p = [p.sb("p2b", [128, 128], BF16) for _ in range(2)]
        pt2a_p = [p.sb("pt2a", [128, 128], BF16) for _ in range(2)]
        pt2b_p = [p.sb("pt2b", [128, 128], BF16) for _ in range(2)]
        ata_p = [p.sb("ata", [128, 128], BF16) for _ in range(2)]
        atb_p = [p.sb("atb", [128, 128], BF16) for _ in range(2)]
        ghat_p = [p.sb("ghat", [128, 128], BF16) for _ in range(2)]
        usb_p = [p.sb("usb", [128, 256]) for _ in range(2)]
        wfm_p = [p.sb("wfm", [128, 256], BF16) for _ in range(2)]
        etm_p = [p.sb("etm", [128, 256]) for _ in range(2)]
        S_f = p.sb("S_f", [128, 2 * 2 * 256])
        S_b = p.sb("S_b", [128, 2 * 2 * 256], BF16)
        gbc = p.sb("gbc", [128, 512])
        grow = p.sb("grow", [1, 512])
        rsc = p.sb("rsc", [128, 512])
        esc = p.sb("esc", [128, 2 * 2 * 512])
        ecar = p.sb("ecar", [128, 4])
        t0b_p = [p.sb("t0b", [128, 256]) for _ in range(2)]
        dsb_p = [p.sb("dsb", [128, 256]) for _ in range(2)]
        obb_p = [p.sb("obb", [128, 256]) for _ in range(2)]
        orb_p = [p.sb("orb", [128, 256], BF16) for _ in range(2)]
        uadj_p = [p.sb("uadj", [128, 256], BF16) for _ in range(2)]
        sqs_p = [p.sb("sqs", [128, 256]) for _ in range(2)]
        sts_p = [p.sb("sts", [128, 4]) for _ in range(2)]
        omf = p.sb("omf", [128, 4 * 512], BF16)

        psA = p.ps("psA", [128, 512])
        psB = p.ps("psB", [128, 512])
        psG_p = [p.ps("psG", [128, 512]) for _ in range(2)]
        psO = p.ps("psO", [128, 512])
        psS = p.ps("psS", [128, 512])
        psT_p = [p.ps("psT", [128, 1024], BF16) for _ in range(2)]
        psT = psT_p[0]

        p.op(V, V.memset, epsa[:, 0:1], 1e-5)
        p.op(V, V.memset, epsa[:, 1:2], 1e-6)
        p.op(V, V.tensor_scalar, nmb_s[:], mb_s[:], -1.0, None, ALU.mult)
        p.op(V, V.memset, carry[:], 0.0)
        p.op(V, V.memset, mcar[:], 0.0)
        p.op(V, V.memset, S_f[:], 0.0)
        p.op(V, V.memset, S_b[:], 0.0)
        p.op(V, V.memset, ecar[:], 0.0)

        import os
        sm1 = os.environ.get("SER1MASK", "000000")
        base_serial = p.force_serial
        def sser(j):
            p.force_serial = base_serial or sm1[j] == "1"
        NG = LSEQ // 512
        for g in range(NG):
            t0 = g * 512
            sser(0)
            for i4 in range(4):
                tt = t0 + i4 * 128
                p.dma(SY, xt[:], io["x"][tt:tt + 128, :])
                p.op(V, V.reduce_sum, stats[:, 0:1], xt[:], AX.X)
                p.op(V, V.tensor_scalar, stats[:, 1:2], stats[:, 0:1], 1.0 / D, None, ALU.mult)
                p.op(V, V.tensor_scalar, xc[:], xt[:], stats[:, 1:2], None, ALU.subtract)
                p.op(V, V.tensor_mul, sqb[:], xc[:], xc[:])
                p.op(V, V.reduce_sum, stats[:, 2:3], sqb[:], AX.X)
                p.op(SC, SC.activation, stats[:, 3:4], stats[:, 2:3], AF.Ln, epsa[:, 0:1], 1.0 / D)
                p.op(SC, SC.activation, stats[:, 4:5], stats[:, 3:4], AF.Exp, scale=-0.5)
                p.op(V, V.tensor_scalar, xlb[:], xc[:], stats[:, 4:5], None, ALU.mult)
                for c in range(8):
                    p.tr(psT[:, c * 128:(c + 1) * 128], xlb[:, c * 128:(c + 1) * 128], C["idb"][:])
                src = psT[:, 0:1024].rearrange("p (c t) -> p c t", c=8)
                dst = xfm[:].rearrange("p (c t) -> p c t", c=8)[:, :, i4 * 128:(i4 + 1) * 128]
                p.op(V, V.tensor_copy, dst, src)
                p.mmgrp([(psS[:, 0:4], xfm[:, c * 512 + i4 * 128: c * 512 + (i4 + 1) * 128],
                          wbg_s[:, c * 4:(c + 1) * 4]) for c in range(8)])
                p.op(SC, SC.activation, sgt[:, 0:4], psS[:, 0:4], AF.Exp, scale=-1.0)
                p.op(V, V.tensor_scalar, sgt[:, 4:8], sgt[:, 0:4], 1.0, None, ALU.add)
                p.op(V, V.reciprocal, bgt[:, i4 * 4:(i4 + 1) * 4], sgt[:, 4:8])
            p.op(V, V.tensor_scalar, nbgt[:], bgt[:], -1.0, None, ALU.mult)

            sser(1)
            p.mmgrp([(psS[0:2, 0:512], wmx_s[:, c * 2:(c + 1) * 2],
                      xfm[:, c * 512:(c + 1) * 512]) for c in range(8)])
            p.op(V, V.tensor_copy, mpad[0:2, 0:3], mcar[:])
            p.op(V, V.tensor_copy, mpad[0:2, 3:515], psS[0:2, 0:512])
            p.op(V, V.tensor_copy, mcar[:], mpad[0:2, 512:515])
            p.op(V, V.tensor_scalar, mixf[:], mpad[0:2, 0:512], cmx_s[0:2, 0:1], None, ALU.mult)
            for j in range(1, 4):
                p.op(V, V.scalar_tensor_tensor, mixf[:], mpad[0:2, j:j + 512],
                     cmx_s[0:2, j:j + 1], mixf[:], ALU.mult, ALU.add)
            p.op(SC, SC.activation, tmpf[0:2, 0:512], mixf[:], AF.Silu)
            p.op(SC, SC.activation, mixe[:], tmpf[0:2, 0:512], AF.Exp, nmb_s[0:2, 0:1], -1.0)
            p.op(V, V.tensor_scalar, mixe[:], mixe[:], 1.0, None, ALU.add)
            p.op(V, V.reciprocal, mixf[:], mixe[:])
            for i4 in range(4):
                p.tr(psS[:, 504:506], mixf[0:2, i4 * 128:(i4 + 1) * 128], C["idf"][0:2, 0:2])
                p.op(V, V.tensor_copy, mixt[:, i4 * 2:(i4 + 1) * 2], psS[:, 504:506])

            sser(2)
            for si, (wsb, csb, outb) in enumerate(((wq_s, cq_s, qfm), (wk_s, ck_s, kfm), (wv_s, cv_s, vfm))):
                for c4 in range(4):
                    p.mmgrp([(psA[:, 0:512],
                              wsb[:, cin * 512 + c4 * 128: cin * 512 + (c4 + 1) * 128],
                              xfm[:, cin * 512:(cin + 1) * 512]) for cin in range(8)])
                    cslot = si * 4 + c4
                    p.op(V, V.tensor_copy, pad[:, 0:3], carry[:, cslot * 3:(cslot + 1) * 3])
                    p.op(V, V.tensor_copy, pad[:, 3:515], psA[:, 0:512])
                    p.op(V, V.tensor_copy, carry[:, cslot * 3:(cslot + 1) * 3], pad[:, 512:515])
                    p.op(V, V.tensor_scalar, cvt[:], pad[:, 0:512], csb[:, c4 * 4:c4 * 4 + 1], None, ALU.mult)
                    for j in range(1, 4):
                        p.op(V, V.scalar_tensor_tensor, cvt[:], pad[:, j:j + 512],
                             csb[:, c4 * 4 + j:c4 * 4 + j + 1], cvt[:], ALU.mult, ALU.add)
                    if si == 2:
                        p.op(SC, SC.activation, outb[:, c4 * 512:(c4 + 1) * 512], cvt[:], AF.Silu)
                    else:
                        p.op(SC, SC.activation, tmpf[:], cvt[:], AF.Silu)
                        p.op(SC, SC.activation, outb[:, c4 * 512:(c4 + 1) * 512], tmpf[:], AF.Silu)

            sser(3)
            for (fmb, dstb) in ((qfm, alp), (kfm, kgm)):
                for h in range(2):
                    for i4 in range(4):
                        p.op(SC, SC.activation, sqf[:, 0:128],
                             fmb[:, (2 * h) * 512 + i4 * 128:(2 * h) * 512 + (i4 + 1) * 128], AF.Square)
                        p.op(SC, SC.activation, sqf[:, 128:256],
                             fmb[:, (2 * h + 1) * 512 + i4 * 128:(2 * h + 1) * 512 + (i4 + 1) * 128], AF.Square)
                        p.mmgrp([(psS[:, 8 + i4 * 2 + h: 9 + i4 * 2 + h],
                                  sqf[:, d * 128:(d + 1) * 128], C["onescol_f"][:]) for d in range(2)])
                p.op(SC, SC.activation, stats[:, 0:8], psS[:, 8:16], AF.Ln, epsa[:, 1:2])
                p.op(SC, SC.activation, dstb[:], stats[:, 0:8], AF.Exp, scale=-0.5)
            for col in range(8):
                i4, h = col // 2, col % 2
                p.op(V, V.tensor_mul, bkg[:, col:col + 1], kgm[:, col:col + 1],
                     bgt[:, i4 * 4 + h:i4 * 4 + h + 1])

            sser(4)
            for h in range(2):
                for i4 in range(4):
                    p.tr(psS[0:1, 256:384], bgt[:, i4 * 4 + 2 + h:i4 * 4 + 3 + h], C["idf"][:])
                    p.op(V, V.tensor_copy, grow[0:1, i4 * 128:(i4 + 1) * 128], psS[0:1, 256:384])
                p.mm(psB[:, 0:512], C["onesf"][:], grow[:], start=True, stop=True)
                p.op(V, V.tensor_copy, gbc[:], psB[:, 0:512])
                for dch in range(2):
                    vsl = vfm[:, (2 * h + dch) * 512:(2 * h + dch + 1) * 512]
                    p.op(V, V.tensor_mul, rsc[:], gbc[:], vsl)
                    p.op(V, V.tensor_sub, rsc[:], vsl, rsc[:])
                    eslot = (h * 2 + dch)
                    p.op(V, V.tensor_tensor_scan, esc[:, eslot * 512:(eslot + 1) * 512],
                         gbc[:], rsc[:], ecar[:, eslot:eslot + 1], ALU.mult, ALU.add)
                    p.op(V, V.tensor_copy, ecar[:, eslot:eslot + 1],
                         esc[:, eslot * 512 + 511:eslot * 512 + 512])

            sser(5)
            def pre_chunk(i4, h):
                cs = (i4 * 2 + h) % 2
                psT = psT_p[cs]
                psG = psG_p[cs]
                kgcol = kgm[:, i4 * 2 + h:i4 * 2 + h + 1]
                bkgcol = bkg[:, i4 * 2 + h:i4 * 2 + h + 1]
                bcol = bgt[:, i4 * 4 + h:i4 * 4 + h + 1]
                nbcol = nbgt[:, i4 * 4 + h:i4 * 4 + h + 1]
                ksl = lambda d: kfm[:, (2 * h + d) * 512 + i4 * 128:(2 * h + d) * 512 + (i4 + 1) * 128]
                vsl = lambda d: vfm[:, (2 * h + d) * 512 + i4 * 128:(2 * h + d) * 512 + (i4 + 1) * 128]
                qsl = lambda d: qfm[:, (2 * h + d) * 512 + i4 * 128:(2 * h + d) * 512 + (i4 + 1) * 128]
                khat, bkh, bvt, khf = khat_p[cs], bkh_p[cs], bvt_p[cs], khf_p[cs]
                nsb, ntb, ghat = nsb_p[cs], ntb_p[cs], ghat_p[cs]
                p2a, p2b, pt2a, pt2b = p2a_p[cs], p2b_p[cs], pt2a_p[cs], pt2b_p[cs]
                ata, atb = ata_p[cs], atb_p[cs]
                for d in range(2):
                    p.tr(psT[:, d * 128:(d + 1) * 128], ksl(d), C["idb"][:])
                yield
                p.op(SC, SC.activation, khat[:], psT[:, 0:256], AF.Identity, scale=kgcol)
                yield
                p.op(SC, SC.activation, bkh[:], psT[:, 0:256], AF.Identity, scale=bkgcol)
                yield
                for d in range(2):
                    p.tr(psT[:, 256 + d * 128:256 + (d + 1) * 128], vsl(d), C["idb"][:])
                yield
                p.op(SC, SC.activation, bvt[:], psT[:, 256:512], AF.Identity, scale=bcol)
                yield
                for d in range(2):
                    p.tr(psT[:, 512 + d * 128:512 + (d + 1) * 128],
                         khat[:, d * 128:(d + 1) * 128], C["idb"][:])
                yield
                p.op(SC, SC.activation, khf[:], psT[:, 512:768], AF.Identity)
                yield
                p.mmgrp([(psG[:, 0:128], khf[:, d * 128:(d + 1) * 128],
                          khf[:, d * 128:(d + 1) * 128]) for d in range(2)])
                yield
                p.op(V, V.scalar_tensor_tensor, nsb[:], psG[:, 0:128], nbcol,
                     C["msl"][:], ALU.mult, ALU.mult)
                yield
                p.tr(psT[:, 768:896], nsb[:], C["idb"][:])
                yield
                p.op(SC, SC.activation, ntb[:], psT[:, 768:896], AF.Identity)
                yield
                p.op(V, V.tensor_add, ata[:], ntb[:], C["idf"][:])
                yield
                Pc, Ptc, Atc = nsb, ntb, ata
                Pn, Ptn, Atn = p2a, pt2a, atb
                e = 2
                tog = 0
                while e <= 64:
                    p.mm(psG[:, 0:128], Ptc[:], Pc[:], start=True, stop=True)
                    yield
                    p.op(SC, SC.activation, Pn[:], psG[:, 0:128], AF.Identity)
                    yield
                    if e < 64:
                        p.mm(psG[:, 128:256], Pc[:], Ptc[:], start=True, stop=True)
                        yield
                        p.op(V, V.tensor_copy, Ptn[:], psG[:, 128:256])
                        yield
                    p.mmgrp([(psG[:, 256:384], Pn[:], Atc[:]),
                             (psG[:, 256:384], C["idb"][:], Atc[:])])
                    yield
                    if tog == 0:
                        p.op(V, V.tensor_copy, Atn[:], psG[:, 256:384])
                    else:
                        p.op(SC, SC.activation, Atn[:], psG[:, 256:384], AF.Identity)
                    yield
                    tog ^= 1
                    Pc, Pn = Pn, (p2b if Pn is p2a else p2a)
                    Ptc, Ptn = Ptn, (pt2b if Ptn is pt2a else pt2a)
                    Atc, Atn = Atn, (ata if Atn is atb else atb)
                    e *= 2
                At = Atc
                p.mm(psA[:, 0:256], At[:], bvt[:], start=True, stop=True)
                yield
                p.op(V, V.tensor_copy, usb_p[cs][:], psA[:, 0:256])
                yield
                for d in range(2):
                    p.mm(psA[:, 256 + d * 128:256 + (d + 1) * 128],
                         bkh[:, d * 128:(d + 1) * 128], At[:], start=True, stop=True)
                yield
                p.op(SC, SC.activation, wfm_p[cs][:], psA[:, 256:512], AF.Identity)
                yield
                p.mmgrp([(psG[:, 0:128], khf[:, d * 128:(d + 1) * 128], qsl(d))
                         for d in range(2)])
                yield
                p.op(V, V.tensor_mul, ghat[:], psG[:, 0:128], C["mui"][:])
                yield
                for d in range(2):
                    p.tr(psS[:, 256 + d * 128:256 + (d + 1) * 128],
                         esc[:, (h * 2 + d) * 512 + i4 * 128:(h * 2 + d) * 512 + (i4 + 1) * 128],
                         C["idf"][:])
                yield
                p.op(V, V.tensor_copy, etm_p[cs][:], psS[:, 256:512])
                yield

            def scan_chunk(i4, h):
                cs = (i4 * 2 + h) % 2
                psT = psT_p[cs]
                acol = alp[:, i4 * 2 + h:i4 * 2 + h + 1]
                mcol = mixt[:, i4 * 2 + h:i4 * 2 + h + 1]
                uadj, t0b, dsb = uadj_p[cs], t0b_p[cs], dsb_p[cs]
                obb, orb, sqs, sts = obb_p[cs], orb_p[cs], sqs_p[cs], sts_p[cs]
                Ssl = lambda d: S_f[:, (h * 2 + d) * 256:(h * 2 + d + 1) * 256]
                Sbl = lambda d: S_b[:, (h * 2 + d) * 256:(h * 2 + d + 1) * 256]
                qsl = lambda d: qfm[:, (2 * h + d) * 512 + i4 * 128:(2 * h + d) * 512 + (i4 + 1) * 128]
                khat, khf, ghat = khat_p[cs], khf_p[cs], ghat_p[cs]
                p.mmgrp([(psB[:, 0:256], wfm_p[cs][:, d * 128:(d + 1) * 128], Sbl(d))
                         for d in range(2)])
                yield
                p.op(V, V.tensor_sub, uadj[:], usb_p[cs][:], psB[:, 0:256])
                yield
                o_calls = [(psO[:, 0:256], qsl(d), Sbl(d)) for d in range(2)]
                o_calls.append((psO[:, 0:256], ghat[:], uadj[:]))
                p.mmgrp(o_calls)
                yield
                for d in range(2):
                    p.mm(psB[:, d * 256:(d + 1) * 256], khat[:, d * 128:(d + 1) * 128],
                         uadj[:], start=True, stop=True)
                yield
                for d in range(2):
                    p.op(V, V.tensor_add, Ssl(d), Ssl(d), psB[:, d * 256:(d + 1) * 256])
                    yield
                    p.op(V, V.tensor_copy, Sbl(d), Ssl(d))
                    yield
                p.op(V, V.tensor_scalar, t0b[:], psO[:, 0:256], acol, None, ALU.mult)
                yield
                p.op(V, V.tensor_sub, dsb[:], etm_p[cs][:], t0b[:])
                yield
                p.op(V, V.scalar_tensor_tensor, obb[:], dsb[:], mcol, t0b[:], ALU.mult, ALU.add)
                yield
                p.op(V, V.tensor_mul, sqs[:], obb[:], obb[:])
                yield
                p.op(V, V.reduce_sum, sts[:, 0:1], sqs[:], AX.X)
                yield
                p.op(SC, SC.activation, sts[:, 1:2], sts[:, 0:1], AF.Ln, epsa[:, 0:1], 1.0 / 256)
                yield
                p.op(SC, SC.activation, sts[:, 2:3], sts[:, 1:2], AF.Exp, scale=-0.5)
                yield
                p.op(V, V.tensor_scalar, orb[:], obb[:], sts[:, 2:3], None, ALU.mult)
                yield
                for d in range(2):
                    p.tr(psT[:, d * 128:(d + 1) * 128],
                         orb[:, d * 128:(d + 1) * 128], C["idb"][:])
                    yield
                    p.op(SC, SC.activation,
                         omf[:, (2 * h + d) * 512 + i4 * 128:(2 * h + d) * 512 + (i4 + 1) * 128],
                         psT[:, d * 128:(d + 1) * 128], AF.Identity, scale=rw_s[:, d:d + 1])
                    yield

            def interleave(*gens):
                act = [g for g in gens if g is not None]
                while act:
                    for g in list(act):
                        try:
                            next(g)
                        except StopIteration:
                            act.remove(g)

            chunks = [(i4, h) for i4 in range(4) for h in range(2)]
            interleave(pre_chunk(*chunks[0]))
            for ci in range(len(chunks)):
                nxt = pre_chunk(*chunks[ci + 1]) if ci + 1 < len(chunks) else None
                interleave(scan_chunk(*chunks[ci]), nxt)
            p.force_serial = base_serial
            for c4 in range(4):
                p.dma(GP, io["om_out"](c4, t0), omf[:, c4 * 512:(c4 + 1) * 512])



def emit_l2(nc, p, C, io):
    V, SC, GP, SY = nc.vector, nc.scalar, nc.gpsimd, nc.sync
    NT = T // 128
    with p.scope():
        wo_s = p.sb("wo_s", [128, 8 * 1024], BF16)
        p.dma(GP, wo_s[:].rearrange("p (c n) -> p c n", c=8),
              io["wo"].rearrange("(c p) n -> p c n", p=128))
        wr_s = p.sb("wr_s", [128, 8 * 8])
        p.dma(SY, wr_s[:].rearrange("p (c n) -> p c n", c=8),
              io["wr"].rearrange("(c p) n -> p c n", p=128))
        wrh = p.sb("wrh", [128, 8 * 8], BF16)
        wrl = p.sb("wrl", [128, 8 * 8], BF16)
        p.op(V, V.tensor_copy, wrh[:], wr_s[:])
        p.op(V, V.tensor_sub, wrl[:], wr_s[:], wrh[:])
        epsa = p.sb("epsa", [128, 1])
        omf_s = p.sb("omf_s", [128, 8 * 512], BF16)
        xr_t = p.sb("xr_t", [128, D])
        x1_s = p.sb("x1_s", [128, D])
        xc = p.sb("xc", [128, D])
        sqb = p.sb("sqb", [128, D])
        stats = p.sb("stats", [128, 8])
        xlf = p.sb("xlf", [128, D])
        xhi = p.sb("xhi", [128, D], BF16)
        xlo = p.sb("xlo", [128, D], BF16)
        xhfm = p.sb("xhfm", [128, 8 * T], BF16)
        xlofm = p.sb("xlofm", [128, 8 * 128], BF16)
        m1 = p.sb("m1", [128, 1])
        m2 = p.sb("m2", [128, 1])
        msk1 = p.sb("msk1", [128, 8])
        msk2 = p.sb("msk2", [128, 8])
        l2b = p.sb("l2b", [128, 8])
        pmb = p.sb("pmb", [128, 8])
        p1b = p.sb("p1b", [128, 1])
        mgate = p.sb("mgate", [128, NT * 8])
        p.op(V, V.memset, epsa[:], 1e-5)

        ctxA = p.scope()
        ctxA.__enter__()
        psY = p.ps("psY", [128, 1024])
        psT = p.ps("psT", [128, 1024], BF16)
        psF = p.ps("psF", [128, 512])

        for i in range(NT):
            g, i4 = i // 4, i % 4
            if i4 == 0:
                for c in range(8):
                    p.dma(GP, omf_s[:, c * 512:(c + 1) * 512], io["om_in"](c, g))
            p.dma(GP, xr_t[:], io["xr_row"](i))
            for n0 in range(2):
                p.mmgrp([(psY[:, n0 * 512:(n0 + 1) * 512],
                          omf_s[:, c * 512 + i4 * 128:c * 512 + (i4 + 1) * 128],
                          wo_s[:, c * 1024 + n0 * 512:c * 1024 + (n0 + 1) * 512])
                         for c in range(8)])
            p.op(V, V.tensor_add, x1_s[:], psY[:], xr_t[:])
            p.dma(GP, io["x2_row"](i), x1_s[:])
            p.op(V, V.reduce_sum, stats[:, 0:1], x1_s[:], AX.X)
            p.op(V, V.tensor_scalar, stats[:, 1:2], stats[:, 0:1], 1.0 / D, None, ALU.mult)
            p.op(V, V.tensor_scalar, xc[:], x1_s[:], stats[:, 1:2], None, ALU.subtract)
            p.op(V, V.tensor_mul, sqb[:], xc[:], xc[:])
            p.op(V, V.reduce_sum, stats[:, 2:3], sqb[:], AX.X)
            p.op(SC, SC.activation, stats[:, 3:4], stats[:, 2:3], AF.Ln, epsa[:, 0:1], 1.0 / D)
            p.op(SC, SC.activation, stats[:, 4:5], stats[:, 3:4], AF.Exp, scale=-0.5)
            p.op(V, V.tensor_scalar, xlf[:], xc[:], stats[:, 4:5], None, ALU.mult)
            p.op(V, V.tensor_copy, xhi[:], xlf[:])
            p.op(V, V.tensor_sub, xlo[:], xlf[:], xhi[:])
            for c in range(8):
                p.tr(psT[:, c * 128:(c + 1) * 128], xhi[:, c * 128:(c + 1) * 128], C["idb"][:])
            p.op(V, V.tensor_copy,
                 xhfm[:].rearrange("p (c t) -> p c t", c=8)[:, :, i * 128:(i + 1) * 128],
                 psT[:, 0:1024].rearrange("p (c t) -> p c t", c=8))
            for c in range(8):
                p.tr(psT[:, c * 128:(c + 1) * 128], xlo[:, c * 128:(c + 1) * 128], C["idb"][:])
            p.op(V, V.tensor_copy, xlofm[:], psT[:, 0:1024])
            calls = []
            for c in range(8):
                xh = xhfm[:, c * T + i * 128:c * T + (i + 1) * 128]
                xl = xlofm[:, c * 128:(c + 1) * 128]
                calls.append((psF[:, 0:8], xh, wrh[:, c * 8:(c + 1) * 8]))
                calls.append((psF[:, 0:8], xh, wrl[:, c * 8:(c + 1) * 8]))
                calls.append((psF[:, 0:8], xl, wrh[:, c * 8:(c + 1) * 8]))
            p.mmgrp(calls)
            p.op(V, V.reduce_max, m1[:], psF[:, 0:8], AX.X)
            p.op(V, V.tensor_tensor, msk1[:], psF[:, 0:8], m1[:].to_broadcast([128, 8]), ALU.is_equal)
            p.op(V, V.scalar_tensor_tensor, l2b[:], msk1[:], -1e30, psF[:, 0:8], ALU.mult, ALU.add)
            p.op(V, V.reduce_max, m2[:], l2b[:], AX.X)
            p.op(V, V.tensor_tensor, msk2[:], l2b[:], m2[:].to_broadcast([128, 8]), ALU.is_equal)
            p.op(V, V.tensor_sub, stats[:, 5:6], m1[:], m2[:])
            p.op(SC, SC.activation, stats[:, 6:7], stats[:, 5:6], AF.Exp, scale=-1.0)
            p.op(V, V.tensor_scalar, stats[:, 7:8], stats[:, 6:7], 1.0, None, ALU.add)
            p.op(V, V.reciprocal, p1b[:], stats[:, 7:8])
            p.op(V, V.tensor_sub, pmb[:], msk1[:], msk2[:])
            p.op(V, V.scalar_tensor_tensor, mgate[:, i * 8:(i + 1) * 8], pmb[:], p1b[:],
                 msk2[:], ALU.mult, ALU.add)

        ctxA.__exit__(None, None, None)
        p.floor_fence()
        with p.scope():
            w1b = [p.sb("w1b", [128, 8 * 1024], BF16) for _ in range(2)]
            w2b = [p.sb("w2b", [128, 8 * 1024], BF16) for _ in range(2)]
            psHp = [p.ps("psHp", [128, 1024]) for _ in range(2)]
            psYp = [p.ps("psYp", [128, 1024]) for _ in range(2)]
            rsbp = [p.sb("rsbp", [128, 1024]) for _ in range(2)]
            hbp = [p.sb("hbp", [128, 1024], BF16) for _ in range(2)]
            ysbp = [p.sb("ysbp", [128, 1024]) for _ in range(2)]

            def ldw(e):
                q = e % 2
                p.dma(GP, w1b[q][:].rearrange("p (c n) -> p c n", c=8),
                      io["w1e"](e).rearrange("(c p) n -> p c n", p=128), bulk=True)
                p.dma(GP, w2b[q][:].rearrange("p (c n) -> p c n", c=8),
                      io["w2e"](e).rearrange("(c p) n -> p c n", p=128), bulk=True)
            ldw(0)
            for e in range(8):
                if e + 1 < 8:
                    ldw(e + 1)
                w1_s, w2_s = w1b[e % 2], w2b[e % 2]
                for i in range(NT):
                    q = i % 2
                    psH, psY = psHp[q], psYp[q]
                    rsb, hb, ysb = rsbp[q], hbp[q], ysbp[q]
                    for dc in range(8):
                        p.mmgrp([(psH[:, dc * 128:(dc + 1) * 128],
                                  w1_s[:, cin * 1024 + dc * 128:cin * 1024 + (dc + 1) * 128],
                                  xhfm[:, cin * T + i * 128:cin * T + (i + 1) * 128])
                                 for cin in range(8)])
                    p.op(SC, SC.activation, rsb[:], psH[:], AF.Relu)
                    p.op(V, V.tensor_mul, hb[:], rsb[:], rsb[:])
                    for n0 in range(2):
                        p.mmgrp([(psY[:, n0 * 512:(n0 + 1) * 512],
                                  hb[:, dc * 128:(dc + 1) * 128],
                                  w2_s[:, dc * 1024 + n0 * 512:dc * 1024 + (n0 + 1) * 512])
                                 for dc in range(8)])
                    if q == 0:
                        p.op(V, V.tensor_scalar, ysb[:], psY[:],
                             mgate[:, i * 8 + e:i * 8 + e + 1], None, ALU.mult)
                    else:
                        p.op(SC, SC.activation, ysb[:], psY[:], AF.Identity,
                             scale=mgate[:, i * 8 + e:i * 8 + e + 1])
                    p.dma(GP, io["x2_row"](i), ysb[:], accum_op=ALU.add)



def emit_l3(nc, p, C, io):
    V, SC, GP, SY = nc.vector, nc.scalar, nc.gpsimd, nc.sync
    NT = T // 128
    with p.scope():
        wqT = p.sb("wqT", [128, 8 * 1024], BF16)
        wkT = p.sb("wkT", [128, 8 * 1024], BF16)
        wvT = p.sb("wvT", [128, 8 * 1024], BF16)
        for scr, w in ((io["scrq"], io["wq1"]), (io["scrk"], io["wk1"]), (io["scrv"], io["wv1"])):
            p.dma(GP, scr.ap(), w)
        for dst, scr in ((wqT, io["scrq"]), (wkT, io["scrk"]), (wvT, io["scrv"])):
            for c in range(8):
                p.dma(SY, dst[:, c * 1024:(c + 1) * 1024],
                      scr.ap()[0:1024, c * 128:(c + 1) * 128], transpose=True)
        bq_s = p.sb("bq_s", [128, 8])
        bk_s = p.sb("bk_s", [128, 8])
        p.dma(SY, bq_s[:], io["bq"])
        p.dma(SY, bk_s[:], io["bk"])
        prj = p.sb("prj", [128, 128], BF16)
        prf = p.sb("prf", [64, 128])
        p.dma(SY, prf[:], io["proj"])
        p.op(V, V.tensor_scalar, prj[0:64, :], prf[:], 1.0 / (8.0 * TWO_PI), None, ALU.mult)
        p.op(V, V.tensor_scalar, prj[64:128, :], prf[:], 1.0 / (8.0 * TWO_PI), None, ALU.mult)

        epsa = p.sb("epsa", [128, 2])
        pih = p.sb("pih", [128, 1])
        mgc = p.sb("mgc", [128, 1])
        xt_p = [p.sb("xt", [128, D]) for _ in range(2)]
        xc_p = [p.sb("xc", [128, D]) for _ in range(2)]
        sqb_p = [p.sb("sqb", [128, D]) for _ in range(2)]
        stats_p = [p.sb("stats", [128, 8]) for _ in range(2)]
        xlb_p = [p.sb("xlb", [128, D], BF16) for _ in range(2)]
        xfm_p = [p.sb("xfm", [128, 8 * 128], BF16) for _ in range(2)]
        qf_g_p = [p.sb("qf_g", [128, 8 * 512], BF16) for _ in range(2)]
        kf_p = [p.sb("kf_s", [128, 8 * 128], BF16) for _ in range(2)]
        vt_p = [p.sb("vt_s", [128, 1024], BF16) for _ in range(2)]
        rmag_p = [p.sb("rmag", [128, 128]) for _ in range(2)]
        rint_p = [p.sb("rint", [128, 128]) for _ in range(2)]
        fsb_p = [p.sb("fsb", [128, 128]) for _ in range(2)]
        fab_p = [p.sb("fab", [128, 128]) for _ in range(2)]
        kp_p = [p.sb("kp_s", [128, 256], BF16) for _ in range(2)]
        onescol_b = p.sb("onescol_b", [128, 1], BF16)
        kvout = p.sb("kvout", [128, 32 * 64])
        skout = p.sb("skout", [128, 32])

        psKV = [p.ps(f"psKV{j}", [128, 512]) for j in range(4)]
        psSK = p.ps("psSK", [128, 512])
        psQ_r = [p.ps(f"psQr{j}", [128, 512]) for j in range(3)]
        rot = [0]

        def nxq():
            b = psQ_r[rot[0] % 3]
            rot[0] += 1
            return b

        p.op(V, V.memset, epsa[:, 0:1], 1e-5)
        p.op(V, V.memset, pih[:], float(np.pi / 2))
        p.op(V, V.memset, mgc[:], MAGICF)
        p.op(V, V.memset, onescol_b[:], 1.0)

        def tile_gen(i):
            g, i4 = i // 4, i % 4
            tp = i % 2
            xt, xc, sqb, stats = xt_p[tp], xc_p[tp], sqb_p[tp], stats_p[tp]
            xlb, xfm, kf_s, vt_s = xlb_p[tp], xfm_p[tp], kf_p[tp], vt_p[tp]
            rmag, rint, fsb, fab, kp_s = (rmag_p[tp], rint_p[tp], fsb_p[tp],
                                          fab_p[tp], kp_p[tp])
            qf_g = qf_g_p[g % 2]
            p.dma(GP, xt[:], io["x2_row"](i))
            yield
            p.op(V, V.reduce_sum, stats[:, 0:1], xt[:], AX.X)
            yield
            p.op(V, V.tensor_scalar, stats[:, 1:2], stats[:, 0:1], 1.0 / D, None, ALU.mult)
            yield
            p.op(V, V.tensor_scalar, xc[:], xt[:], stats[:, 1:2], None, ALU.subtract)
            yield
            p.op(V, V.tensor_mul, sqb[:], xc[:], xc[:])
            yield
            p.op(V, V.reduce_sum, stats[:, 2:3], sqb[:], AX.X)
            yield
            p.op(SC, SC.activation, stats[:, 3:4], stats[:, 2:3], AF.Ln, epsa[:, 0:1], 1.0 / D)
            yield
            p.op(SC, SC.activation, stats[:, 4:5], stats[:, 3:4], AF.Exp, scale=-0.5)
            yield
            p.op(V, V.tensor_scalar, xlb[:], xc[:], stats[:, 4:5], None, ALU.mult)
            yield
            for c in range(8):
                p.dma(SY, xfm[:, c * 128:(c + 1) * 128],
                      xlb[:, c * 128:(c + 1) * 128], transpose=True)
                yield
            for (wT, bias_s, outf) in ((wqT, bq_s, None), (wkT, bk_s, kf_s)):
                for c in range(8):
                    ps = nxq()
                    p.mmgrp([(ps[:, 0:128],
                              wT[:, cin * 1024 + c * 128:cin * 1024 + (c + 1) * 128],
                              xfm[:, cin * 128:(cin + 1) * 128]) for cin in range(8)])
                    yield
                    if outf is None:
                        p.op(SC, SC.activation, qf_g[:, c * 512 + i4 * 128:c * 512 + (i4 + 1) * 128],
                             ps[:, 0:128], AF.Identity, bias_s[:, c:c + 1])
                    else:
                        p.op(SC, SC.activation, outf[:, c * 128:(c + 1) * 128],
                             ps[:, 0:128], AF.Identity, bias_s[:, c:c + 1])
                    yield
            for n0 in range(2):
                ps = nxq()
                p.mmgrp([(ps[:, 0:512],
                          xfm[:, cin * 128:(cin + 1) * 128],
                          wvT[:, cin * 1024 + n0 * 512:cin * 1024 + (n0 + 1) * 512])
                         for cin in range(8)])
                yield
                p.op(V, V.tensor_copy, vt_s[:, n0 * 512:(n0 + 1) * 512], ps[:, 0:512])
                yield
            if i4 == 3:
                for c in range(8):
                    p.dma(GP, io["qf_out"](c, g), qf_g[:, c * 512:(c + 1) * 512])
                    yield
            for h in range(16):
                b0 = (h % 2) * 64
                ps = nxq()
                p.mm(ps[:, 0:128], kf_s[b0:b0 + 64, (h // 2) * 128:(h // 2 + 1) * 128],
                     prj[b0:b0 + 64, :], start=True, stop=True)
                yield
                p.op(SC, SC.activation, rmag[:], ps[:, 0:128], AF.Identity, mgc[:, 0:1])
                yield
                p.op(V, V.tensor_scalar, rint[:], rmag[:], MAGICF, None, ALU.subtract)
                yield
                p.op(V, V.tensor_sub, fsb[:], ps[:, 0:128], rint[:])
                yield
                p.op(SC, SC.activation, kp_s[:, 128:256], fsb[:], AF.Sin, scale=TWO_PI)
                yield
                p.op(SC, SC.activation, fab[:], fsb[:], AF.Abs)
                yield
                p.op(SC, SC.activation, kp_s[:, 0:128], fab[:], AF.Sin, pih[:, 0:1], -TWO_PI)
                yield
                for mc in range(2):
                    r = h * 2 + mc
                    bank, co = r // 8, (r % 8) * 64
                    kv_start = (i == 0 and r % 8 == 0)
                    sk_start = (i == 0 and r == 0)
                    stop = (i == NT - 1)
                    p.mm(psKV[bank][:, co:co + 64], kp_s[:, mc * 128:(mc + 1) * 128],
                         vt_s[:, h * 64:(h + 1) * 64],
                         start=kv_start, stop=stop, skip_group_check=True)
                    p.mm(psSK[:, r:r + 1], kp_s[:, mc * 128:(mc + 1) * 128],
                         onescol_b[:], start=sk_start, stop=stop, skip_group_check=True)
                    yield

        pipeline(tile_gen, NT, 2)
        for bank in range(4):
            p.op(V, V.tensor_copy, kvout[:, bank * 512:(bank + 1) * 512], psKV[bank][:, 0:512])
        p.op(V, V.tensor_copy, skout[:], psSK[:, 0:32])
        for r in range(32):
            p.dma(GP, io["kv_out"](r), kvout[:, r * 64:(r + 1) * 64])
            p.dma(GP, io["sk_out"](r), skout[:, r:r + 1])



def emit_l4(nc, p, C, io):
    V, SC, GP, SY = nc.vector, nc.scalar, nc.gpsimd, nc.sync
    NT = T // 128
    with p.scope():
        be_s = p.sb("be_s", [1, 8 * D])
        p.dma(SY, be_s[:], io["be"])
        xhfm = p.sb("xhfm", [128, 8 * T], BF16)
        mg4 = p.sb("mg4", [128, 32 * 8])
        ctxA = p.scope()
        ctxA.__enter__()
        woT = p.sb("woT", [128, 8 * 1024], BF16)
        p.dma(GP, io["scrw"].ap(), io["wo1"])
        for c in range(8):
            p.dma(SY, woT[:, c * 1024:(c + 1) * 1024],
                  io["scrw"].ap()[0:1024, c * 128:(c + 1) * 128], transpose=True)
        bo_s = p.sb("bo_s", [1, D])
        p.dma(SY, bo_s[:], io["bo"])
        bv_s = p.sb("bv_s", [1, D])
        p.dma(SY, bv_s[:], io["bv"])
        rb_s = p.sb("rb_s", [1, 8])
        p.dma(SY, rb_s[:], io["rb4"])
        wr_s = p.sb("wr_s", [128, 8 * 8])
        p.dma(SY, wr_s[:].rearrange("p (c n) -> p c n", c=8),
              io["wr4"].rearrange("(c p) n -> p c n", p=128))
        wrh = p.sb("wrh", [128, 8 * 8], BF16)
        wrl = p.sb("wrl", [128, 8 * 8], BF16)
        p.op(V, V.tensor_copy, wrh[:], wr_s[:])
        p.op(V, V.tensor_sub, wrl[:], wr_s[:], wrh[:])
        prj = p.sb("prj", [128, 128], BF16)
        prf = p.sb("prf", [64, 128])
        p.dma(SY, prf[:], io["proj"])
        p.op(V, V.tensor_scalar, prj[0:64, :], prf[:], 1.0 / (8.0 * TWO_PI), None, ALU.mult)
        p.op(V, V.tensor_scalar, prj[64:128, :], prf[:], 1.0 / (8.0 * TWO_PI), None, ALU.mult)

        epsa = p.sb("epsa", [128, 1])
        pih = p.sb("pih", [128, 1])
        mgc = p.sb("mgc", [128, 1])
        kva = p.sb("kva", [128, 2048])
        kvb = p.sb("kvb", [128, 2048], BF16)
        ska = p.sb("ska", [128, 32])
        qf_g_p = [p.sb("qf_g", [128, 8 * 512], BF16) for _ in range(2)]
        rmag_p = [p.sb("rmag", [128, 128]) for _ in range(2)]
        rint_s = p.sb("rint", [128, 128])
        rint_p = [rint_s, rint_s]
        fsb_p = [p.sb("fsb", [128, 128]) for _ in range(2)]
        fab_s = p.sb("fab", [128, 128])
        fab_p = [fab_s, fab_s]
        qp_p = [p.sb("qp_s", [128, 256], BF16) for _ in range(2)]
        of_p = [p.sb("of_s", [128, 8 * 128], BF16) for _ in range(2)]
        xr_p = [p.sb("xr_t", [128, D]) for _ in range(2)]
        x3_p = [p.sb("x3_s", [128, D]) for _ in range(2)]
        xc_s = p.sb("xc", [128, D])
        xc_p = [xc_s, xc_s]
        sqb_s = p.sb("sqb", [128, D])
        sqb_p = [sqb_s, sqb_s]
        stats_p = [p.sb("stats", [128, 8]) for _ in range(2)]
        xlf_s = p.sb("xlf", [128, D])
        xlf_p = [xlf_s, xlf_s]
        xhi_p = [p.sb("xhi", [128, D], BF16) for _ in range(2)]
        xlo_p = [p.sb("xlo", [128, D], BF16) for _ in range(2)]
        xlofm_p = [p.sb("xlofm", [128, 8 * 128], BF16) for _ in range(2)]
        m1_p = [p.sb("m1", [128, 1]) for _ in range(2)]
        negm_p = [p.sb("negm", [128, 1]) for _ in range(2)]
        msk_p = [p.sb("msk", [128, 8]) for _ in range(2)]
        exb_p = [p.sb("exb", [128, 8]) for _ in range(2)]
        ssum_p = [p.sb("ssum", [128, 1]) for _ in range(2)]
        gat_p = [p.sb("gat", [128, 1]) for _ in range(2)]

        psR = [p.ps(f"psR4{j}", [128, 512]) for j in range(8)]
        rot = [0]

        def nxr():
            b = psR[rot[0] % 8]
            rot[0] += 1
            return b

        p.op(V, V.memset, epsa[:], 1e-5)
        p.op(V, V.memset, pih[:], float(np.pi / 2))
        p.op(V, V.memset, mgc[:], MAGICF)

        with p.scope():
            kvb_ = p.sb("kvb_", [128, 1024])
            skb = p.sb("skb", [128, 32])
            p.dma(SY, kva[:].rearrange("p (r n) -> p r n", r=32), io["kv0_re"])
            p.dma(SY, ska[:].rearrange("p (r n) -> p r n", r=32), io["sk0_re"],
                  allow_slow_non_contiguous=True)
            p.dma(SY, skb[:].rearrange("p (r n) -> p r n", r=32), io["sk1_re"],
                  allow_slow_non_contiguous=True)
            for half in range(2):
                p.dma(SY, kvb_[:].rearrange("p (r n) -> p r n", r=16),
                      io["kv1_re"][:, half * 16:(half + 1) * 16, :])
                p.op(V, V.tensor_add, kva[:, half * 1024:(half + 1) * 1024],
                     kva[:, half * 1024:(half + 1) * 1024], kvb_[:])
            p.op(V, V.tensor_add, ska[:], ska[:], skb[:])
        for r in range(32):
            h = r // 2
            p.mm(psR[0][:, 0:64], C["onesf"][:], bv_s[0:1, h * 64:(h + 1) * 64],
                 start=True, stop=True)
            p.op(V, V.scalar_tensor_tensor, kva[:, r * 64:(r + 1) * 64], psR[0][:, 0:64],
                 ska[:, r:r + 1], kva[:, r * 64:(r + 1) * 64], ALU.mult, ALU.add)
        p.op(V, V.tensor_copy, kvb[:], kva[:])

        def tile_gen(i):
            g, i4 = i // 4, i % 4
            tp = i % 2
            qf_g = qf_g_p[g % 2]
            rmag, rint, fsb, fab, qp_s = (rmag_p[tp], rint_p[tp], fsb_p[tp],
                                          fab_p[tp], qp_p[tp])
            of_s, xr_t, x3_s, xc, sqb = (of_p[tp], xr_p[tp], x3_p[tp],
                                         xc_p[tp], sqb_p[tp])
            stats, xlf, xhi, xlo, xlofm = (stats_p[tp], xlf_p[tp], xhi_p[tp],
                                           xlo_p[tp], xlofm_p[tp])
            m1, negm, msk = m1_p[tp], negm_p[tp], msk_p[tp]
            exb, ssum, gat = exb_p[tp], ssum_p[tp], gat_p[tp]
            if i4 == 0:
                for c in range(8):
                    p.dma(GP, qf_g[:, c * 512:(c + 1) * 512], io["qf_in"](c, g))
                    yield
            p.dma(GP, xr_t[:], io["x2_row"](i))
            yield
            for h in range(16):
                b0 = (h % 2) * 64
                psq = nxr()
                p.mm(psq[:, 0:128], prj[b0:b0 + 64, :],
                     qf_g[b0:b0 + 64, (h // 2) * 512 + i4 * 128:
                          (h // 2) * 512 + (i4 + 1) * 128], start=True, stop=True)
                yield
                p.op(SC, SC.activation, rmag[:], psq[:, 0:128], AF.Identity, mgc[:, 0:1])
                yield
                p.op(V, V.tensor_scalar, rint[:], rmag[:], MAGICF, None, ALU.subtract)
                p.op(V, V.tensor_sub, fsb[:], psq[:, 0:128], rint[:])
                yield
                p.op(SC, SC.activation, qp_s[:, 128:256], fsb[:], AF.Sin, scale=TWO_PI)
                yield
                p.op(SC, SC.activation, fab[:], fsb[:], AF.Abs)
                p.op(SC, SC.activation, qp_s[:, 0:128], fab[:], AF.Sin, pih[:, 0:1], -TWO_PI)
                yield
                pso = nxr()
                p.mmgrp([(pso[0:64, 0:128], kvb[:, (h * 2 + mc) * 64:(h * 2 + mc + 1) * 64],
                          qp_s[:, mc * 128:(mc + 1) * 128]) for mc in range(2)])
                yield
                p.op(V, V.tensor_scalar, of_s[b0:b0 + 64, (h // 2) * 128:(h // 2 + 1) * 128],
                     pso[0:64, 0:128], 1.0 / 128.0, None, ALU.mult)
                yield
            for n0 in range(2):
                psy = nxr()
                calls = [(psy[:, 0:512],
                          of_s[:, c * 128:(c + 1) * 128],
                          woT[:, c * 1024 + n0 * 512:c * 1024 + (n0 + 1) * 512])
                         for c in range(8)]
                calls.append((psy[:, 0:512], C["onesf"][:],
                              bo_s[0:1, n0 * 512:(n0 + 1) * 512]))
                p.mmgrp(calls)
                yield
                p.op(V, V.tensor_add, x3_s[:, n0 * 512:(n0 + 1) * 512],
                     psy[:, 0:512], xr_t[:, n0 * 512:(n0 + 1) * 512])
                yield
            p.dma(GP, io["xo_row"](i), x3_s[:])
            yield
            p.op(V, V.reduce_sum, stats[:, 0:1], x3_s[:], AX.X)
            yield
            p.op(V, V.tensor_scalar, stats[:, 1:2], stats[:, 0:1], 1.0 / D, None, ALU.mult)
            yield
            p.op(V, V.tensor_scalar, xc[:], x3_s[:], stats[:, 1:2], None, ALU.subtract)
            p.op(V, V.tensor_mul, sqb[:], xc[:], xc[:])
            p.op(V, V.reduce_sum, stats[:, 2:3], sqb[:], AX.X)
            p.op(SC, SC.activation, stats[:, 3:4], stats[:, 2:3], AF.Ln, epsa[:, 0:1], 1.0 / D)
            p.op(SC, SC.activation, stats[:, 4:5], stats[:, 3:4], AF.Exp, scale=-0.5)
            p.op(V, V.tensor_scalar, xlf[:], xc[:], stats[:, 4:5], None, ALU.mult)
            p.op(V, V.tensor_copy, xhi[:], xlf[:])
            p.op(V, V.tensor_sub, xlo[:], xlf[:], xhi[:])
            yield
            for c in range(8):
                p.dma(SY, xhfm[:, c * T + i * 128:c * T + (i + 1) * 128],
                      xhi[:, c * 128:(c + 1) * 128], transpose=True)
                yield
            for c in range(8):
                p.dma(SY, xlofm[:, c * 128:(c + 1) * 128],
                      xlo[:, c * 128:(c + 1) * 128], transpose=True)
                yield
            psf = nxr()
            calls = []
            for c in range(8):
                xh = xhfm[:, c * T + i * 128:c * T + (i + 1) * 128]
                xl = xlofm[:, c * 128:(c + 1) * 128]
                calls.append((psf[:, 0:8], xh, wrh[:, c * 8:(c + 1) * 8]))
                calls.append((psf[:, 0:8], xh, wrl[:, c * 8:(c + 1) * 8]))
                calls.append((psf[:, 0:8], xl, wrh[:, c * 8:(c + 1) * 8]))
            calls.append((psf[:, 0:8], C["onesf"][:], rb_s[0:1, 0:8]))
            p.mmgrp(calls)
            yield
            p.op(V, V.reduce_max, m1[:], psf[:, 0:8], AX.X)
            yield
            p.op(V, V.tensor_tensor, msk[:], psf[:, 0:8],
                 m1[:].to_broadcast([128, 8]), ALU.is_equal)
            yield
            p.op(V, V.tensor_scalar, negm[:], m1[:], -1.0, None, ALU.mult)
            yield
            p.op(SC, SC.activation, exb[:], psf[:, 0:8], AF.Exp, negm[:, 0:1])
            yield
            p.op(V, V.reduce_sum, ssum[:], exb[:], AX.X)
            yield
            p.op(V, V.reciprocal, gat[:], ssum[:])
            yield
            p.op(V, V.tensor_scalar, mg4[:, i * 8:(i + 1) * 8], msk[:], gat[:, 0:1], None, ALU.mult)
            yield

        pipeline(tile_gen, NT, 2)

        ctxA.__exit__(None, None, None)
        p.floor_fence()
        with p.scope():
            web = [p.sb("web", [128, 8 * 1024], BF16) for _ in range(2)]
            psYq = [p.ps("psYq", [128, 1024]) for _ in range(2)]
            ysbq = [p.sb("ysbq", [128, 1024]) for _ in range(2)]

            def ldwe(e):
                p.dma(GP, web[e % 2][:].rearrange("p (c n) -> p c n", c=8),
                      io["we_e"](e).rearrange("(c p) n -> p c n", p=128), bulk=True)
            ldwe(0)
            for e in range(8):
                if e + 1 < 8:
                    ldwe(e + 1)
                we_s = web[e % 2]
                for i in range(NT):
                    q = i % 2
                    psYe, ysbe = psYq[q], ysbq[q]
                    for n0 in range(2):
                        calls = [(psYe[:, n0 * 512:(n0 + 1) * 512],
                                  xhfm[:, cin * T + i * 128:cin * T + (i + 1) * 128],
                                  we_s[:, cin * 1024 + n0 * 512:cin * 1024 + (n0 + 1) * 512])
                                 for cin in range(8)]
                        calls.append((psYe[:, n0 * 512:(n0 + 1) * 512], C["onesf"][:],
                                      be_s[0:1, e * 1024 + n0 * 512:e * 1024 + (n0 + 1) * 512]))
                        p.mmgrp(calls)
                    if q == 0:
                        p.op(V, V.tensor_scalar, ysbe[:], psYe[:],
                             mg4[:, i * 8 + e:i * 8 + e + 1], None, ALU.mult)
                    else:
                        p.op(SC, SC.activation, ysbe[:], psYe[:], AF.Identity,
                             scale=mg4[:, i * 8 + e:i * 8 + e + 1])
                    p.dma(GP, io["xo_row"](i), ysbe[:], accum_op=ALU.add)



def build_fused():
    nc = bass.Bass("TRN2", target_bir_lowering=False, debug=False)
    ein = lambda n, s, dt=F32: nc.dram_tensor(n, s, dt, kind="ExternalInput").ap()
    x = ein("x", [LSEQ, D])
    wq = ein("wq", [D, 512]); wk = ein("wk", [D, 512]); wv = ein("wv", [D, 512])
    cq = ein("cq", [512, 4]); ck = ein("ck", [512, 4]); cv = ein("cv", [512, 4])
    wbg = ein("wbg", [D, 4]); wmx = ein("wmx", [D, 2]); cmx = ein("cmx", [2, 4])
    mbias = ein("mbias", [2, 1]); rmsw = ein("rmsw", [128, 2])
    wo = ein("wo", [D, D]); wr = ein("wr", [D, 8])
    w1 = ein("w1", [8, D, D]); w2 = ein("w2", [8, D, D])
    wq1 = ein("wq1", [D, D]); wk1 = ein("wk1", [D, D]); wv1 = ein("wv1", [D, D])
    bq = ein("bq", [128, 8]); bk = ein("bk", [128, 8]); proj = ein("proj", [64, 128])
    bv = ein("bv", [1, D]); wo1 = ein("wo1", [D, D]); bo = ein("bo", [1, D])
    wr4 = ein("wr4", [D, 8]); rb4 = ein("rb4", [1, 8])
    we = ein("we", [8, D, D]); be = ein("be", [1, 8 * D])
    slot = ein("slot", [1, 1], I32); nonce = ein("nonce", [1, 1], I32)
    xo = nc.dram_tensor("xo", [T, D], F32, kind="ExternalOutput").ap()

    om_sh = nc.dram_tensor("om_sh", [2 * 4 * 16 * 128, 512], BF16, kind="Internal", addr_space="Shared")
    kv_sh = nc.dram_tensor("kv_sh", [2 * 4096, 64], F32, kind="Internal", addr_space="Shared")
    sk_sh = nc.dram_tensor("sk_sh", [2 * 4096, 1], F32, kind="Internal", addr_space="Shared")
    fl_sh = nc.dram_tensor("fl_sh", [2 * 2, 1], I32, kind="Internal", addr_space="Shared")
    x2_i = nc.dram_tensor("x2_i", [T, D], F32, kind="Internal")
    qf_i = nc.dram_tensor("qf_i", [D, T], BF16, kind="Internal")
    scrq = nc.dram_tensor("scrq", [D, D], BF16, kind="Internal")
    scrk = nc.dram_tensor("scrk", [D, D], BF16, kind="Internal")
    scrv = nc.dram_tensor("scrv", [D, D], BF16, kind="Internal")
    scrw = nc.dram_tensor("scrw", [D, D], BF16, kind="Internal")

    import os
    with ExitStack() as ctx:
        p = (Pr if os.environ.get("SERIAL_SCHED") else Pr2)(nc, ctx)
        C = _consts(nc, p)
        V, SC, GP, SY = nc.vector, nc.scalar, nc.gpsimd, nc.sync

        slt = p.sb("slt", [1, 2], I32)
        flt = p.sb("flt", [1, 1], I32)
        psem = ctx.enter_context(nc.semaphore("psem"))
        p.dma(SY, slt[0:1, 0:1], slot)
        p.dma(SY, slt[0:1, 1:2], nonce)
        p._w(GP)
        p._w(SY)
        p._w(SC)
        rs_r = GP.alloc_register("rs_r")
        rn_r = GP.alloc_register("rn_r")
        rf_r = GP.alloc_register("rf_r")
        rc_r = GP.alloc_register("rc_r")
        cnd_r = GP.alloc_register("cnd_r")
        GP.reg_load(rs_r, slt[0:1, 0:1])
        GP.reg_load(rn_r, slt[0:1, 1:2])
        GP.reg_mov(rc_r, 0)
        p.slot_reg = rs_r
        rs_sy = SY.alloc_register("rs_sy")
        SY.reg_load(rs_sy, slt[0:1, 0:1])
        rs_sc = SC.alloc_register("rs_sc")
        SC.reg_load(rs_sc, slt[0:1, 0:1])
        p.slot_regs = {"gp": rs_r, "sync": rs_sy, "sc": rs_sc}

        def poll_partner(col, partner):
            GP.reg_mov(cnd_r, 1)
            with GP.While(cnd_r):
                GP.dma_start(out=flt[0:1, 0:1],
                             in_=fl_sh.ap()[partner * 2 + col:partner * 2 + col + 1, :]
                             ).then_inc(psem, 16)
                GP.reg_add(rc_r, rc_r, 16)
                GP.wait_ge(psem, rc_r)
                GP.reg_load(rf_r, flt[0:1, 0:1])
                GP.reg_alu(cnd_r, rf_r, rn_r, ALU.not_equal)

        def barrier(col):
            p._w(GP)
            p.dma(GP, (fl_sh.ap()[col:col + 1, :], fl_sh.ap()[2 + col:3 + col, :]), nonce)
            p._w(GP)
            with GP.If_eq(rs_r, 0):
                poll_partner(col, 1)
            with GP.Else():
                poll_partner(col, 0)
            p.set_fence(GP)

        io1 = {
            "x": x, "wq": wq, "wk": wk, "wv": wv, "cq": cq, "ck": ck, "cv": cv,
            "wbg": wbg, "wmx": wmx, "cmx": cmx, "mbias": mbias, "rmsw": rmsw,
            "om_out": lambda c4, t0: tuple(
                om_sh.ap()[((sl * 4 + c4) * 16 + t0 // 512) * 128:
                           ((sl * 4 + c4) * 16 + t0 // 512) * 128 + 128, :]
                for sl in (0, 1)),
        }
        import os
        smask = os.environ.get("SERMASK", "0000")
        p.force_serial = smask[0] == "1"
        emit_l1(nc, p, C, io1)
        p.force_serial = False
        barrier(0)

        io2 = {
            "om_in": lambda c, g: tuple(
                om_sh.ap()[(c // 4) * 8192 + (c % 4) * 2048 + sl * 1024 + g * 128:
                           (c // 4) * 8192 + (c % 4) * 2048 + sl * 1024 + g * 128 + 128, :]
                for sl in (0, 1)),
            "xr_row": lambda i: tuple(
                x[sl * T + i * 128:sl * T + (i + 1) * 128, :] for sl in (0, 1)),
            "x2_row": lambda i: x2_i.ap()[i * 128:(i + 1) * 128, :],
            "wo": wo, "wr": wr,
            "w1e": lambda e: w1[e], "w2e": lambda e: w2[e],
        }
        p.force_serial = smask[1] == "1"
        emit_l2(nc, p, C, io2)
        p.force_serial = False
        p.floor_fence()

        io3 = {
            "x2_row": lambda i: x2_i.ap()[i * 128:(i + 1) * 128, :],
            "wq1": wq1, "wk1": wk1, "wv1": wv1, "bq": bq, "bk": bk, "proj": proj,
            "scrq": scrq, "scrk": scrk, "scrv": scrv,
            "kv_out": lambda r: tuple(
                kv_sh.ap()[sl * 4096 + r * 128:sl * 4096 + (r + 1) * 128, :] for sl in (0, 1)),
            "sk_out": lambda r: tuple(
                sk_sh.ap()[sl * 4096 + r * 128:sl * 4096 + (r + 1) * 128, :] for sl in (0, 1)),
            "qf_out": lambda c, g: qf_i.ap()[c * 128:(c + 1) * 128, g * 512:(g + 1) * 512],
        }
        p.force_serial = smask[2] == "1"
        emit_l3(nc, p, C, io3)
        p.force_serial = False
        barrier(1)

        io4 = {
            "kv0_re": kv_sh.ap()[0:4096, :].rearrange("(r p) n -> p r n", p=128),
            "kv1_re": kv_sh.ap()[4096:8192, :].rearrange("(r p) n -> p r n", p=128),
            "sk0_re": sk_sh.ap()[0:4096, :].rearrange("(r p) n -> p r n", p=128),
            "sk1_re": sk_sh.ap()[4096:8192, :].rearrange("(r p) n -> p r n", p=128),
            "qf_in": lambda c, g: qf_i.ap()[c * 128:(c + 1) * 128, g * 512:(g + 1) * 512],
            "x2_row": lambda i: x2_i.ap()[i * 128:(i + 1) * 128, :],
            "bv": bv, "wo1": wo1, "bo": bo, "wr4": wr4, "rb4": rb4, "be": be,
            "proj": proj, "scrw": scrw,
            "we_e": lambda e: we[e],
            "xo_row": lambda i: xo[i * 128:(i + 1) * 128, :],
        }
        p.force_serial = smask[3] == "1"
        emit_l4(nc, p, C, io4)
        p.force_serial = False
        p.fin(GP)
    return nc


def kernel(**inputs):
    I = {k: np.asarray(v) for k, v in inputs.items()}
    x0 = np.ascontiguousarray(I["L0_attn_hidden_states"].astype(np.float32))
    cores = list(range(8))
    nonce = np.int32((time.time_ns() // 1000) & 0x7FFFFFFF)

    nc = build_fused()
    maps = []
    for c in cores:
        b, sl = c // 2, c % 2
        hsl = slice(sl * 512, (sl + 1) * 512)
        hh = slice(2 * sl, 2 * sl + 2)
        maps.append({
            "x": x0[b],
            "wq": np.ascontiguousarray(I["L0_attn_Wq"][:, hsl]),
            "wk": np.ascontiguousarray(I["L0_attn_Wk"][:, hsl]),
            "wv": np.ascontiguousarray(I["L0_attn_Wv"][:, hsl]),
            "cq": np.ascontiguousarray(I["L0_attn_conv_q"][hsl]),
            "ck": np.ascontiguousarray(I["L0_attn_conv_k"][hsl]),
            "cv": np.ascontiguousarray(I["L0_attn_conv_v"][hsl]),
            "wbg": np.ascontiguousarray(np.concatenate(
                [I["L0_attn_Wb"][:, hh], I["L0_attn_Wdec"][:, hh]], axis=1)),
            "wmx": np.ascontiguousarray(I["L0_attn_Wmix"][:, hh]),
            "cmx": np.ascontiguousarray(I["L0_attn_conv_mix"][hh]),
            "mbias": np.ascontiguousarray(I["L0_attn_mix_bias"][hh][:, None]),
            "rmsw": np.ascontiguousarray(I["L0_attn_rms_w"].reshape(2, 128).T),
            "wo": I["L0_attn_Wo"].astype(np.float32),
            "wr": I["L0_ffn_w_router"].astype(np.float32),
            "w1": I["L0_ffn_w1"].astype(np.float32),
            "w2": I["L0_ffn_w2"].astype(np.float32),
            "wq1": I["L1_attn_Wq"].astype(np.float32),
            "wk1": I["L1_attn_Wk"].astype(np.float32),
            "wv1": I["L1_attn_Wv"].astype(np.float32),
            "bq": np.ascontiguousarray(I["L1_attn_bq"].astype(np.float32).reshape(8, 128).T),
            "bk": np.ascontiguousarray(I["L1_attn_bk"].astype(np.float32).reshape(8, 128).T),
            "proj": I["L1_attn_proj_mat"].astype(np.float32),
            "bv": np.ascontiguousarray(I["L1_attn_bv"].astype(np.float32).reshape(1, D)),
            "wo1": I["L1_attn_Wo"].astype(np.float32),
            "bo": np.ascontiguousarray(I["L1_attn_bo"].astype(np.float32).reshape(1, D)),
            "wr4": I["L1_ffn_router_w"].astype(np.float32),
            "rb4": np.ascontiguousarray(I["L1_ffn_router_b"].astype(np.float32).reshape(1, 8)),
            "we": I["L1_ffn_expert_w"].astype(np.float32),
            "be": np.ascontiguousarray(I["L1_ffn_expert_b"].astype(np.float32).reshape(1, 8 * D)),
            "slot": np.array([[sl]], np.int32),
            "nonce": np.array([[nonce]], np.int32),
        })
    res = run_bass_kernel_spmd(nc, maps, core_ids=cores)

    out = np.empty((4, LSEQ, D), dtype=np.float32)
    for c in cores:
        b, sl = c // 2, c % 2
        out[b, sl * T:(sl + 1) * T] = res.results[c]["xo"]
    return out


# version 21 (final)
# speedup vs baseline: 1.1784x; 1.1784x over previous
import numpy as np
import ml_dtypes
import time
from contextlib import ExitStack, contextmanager

import concourse.bass as bass
import concourse.mybir as mybir
from concourse.bass_utils import run_bass_kernel_spmd
from concourse.expressions import RuntimeValue

F32 = mybir.dt.float32
BF16 = mybir.dt.bfloat16
I32 = mybir.dt.int32
AF = mybir.ActivationFunctionType
ALU = mybir.AluOpType
AX = mybir.AxisListType
NBF = ml_dtypes.bfloat16
ds = bass.ds

D = 1024
LSEQ = 8192
T = 4096
TWO_PI = 6.283185307179586
MAGICF = float(np.float32(1.5 * 2 ** 23))


class Pr:

    def __init__(self, nc, ctx):
        self.nc = nc
        self.ctx = ctx
        self.C = ctx.enter_context(nc.semaphore("C"))
        self.Dm = ctx.enter_context(nc.semaphore("Dm"))
        self.c = 0
        self.d = 0
        self.nn = 0

    @contextmanager
    def scope(self):
        old = self.ctx
        with ExitStack() as es:
            self.ctx = es
            try:
                yield
            finally:
                self.ctx = old

    def sb(self, name, shape, dt=F32):
        self.nn += 1
        return self.ctx.enter_context(self.nc.sbuf_tensor(f"{name}_{self.nn}", shape, dt))

    def ps(self, name, shape, dt=F32):
        self.nn += 1
        return self.ctx.enter_context(self.nc.psum_tensor(f"{name}_{self.nn}", shape, dt))

    def _w(self, eng):
        eng.wait_ge(self.C, self.c)
        eng.wait_ge(self.Dm, 16 * self.d)

    def op(self, eng, fn, *a, **k):
        self._w(eng)
        ins = fn(*a, **k)
        self.c += 1
        ins.then_inc(self.C, 1)
        return ins

    def mm(self, *a, **k):
        t = self.nc.tensor
        self._w(t)
        ins = t.matmul(*a, **k)
        self.c += 1
        ins.then_inc(self.C, 1)
        return ins

    def mmgrp(self, calls):
        t = self.nc.tensor
        self._w(t)
        n = len(calls)
        for i, (o, l, r) in enumerate(calls):
            ins = t.matmul(o, l, r, start=(i == 0), stop=(i == n - 1))
        self.c += 1
        ins.then_inc(self.C, 1)

    def tr(self, out_psum, in_sbuf, ident):
        t = self.nc.tensor
        self._w(t)
        ins = t.transpose(out_psum, in_sbuf, ident)
        self.c += 1
        ins.then_inc(self.C, 1)

    def dma(self, eng, out, in_, **k):
        k.pop("bulk", None)
        self._w(eng)
        if isinstance(out, tuple) or isinstance(in_, tuple):
            o0, o1 = out if isinstance(out, tuple) else (out, out)
            i0, i1 = in_ if isinstance(in_, tuple) else (in_, in_)
            with eng.If_eq(self.slot_reg, 0):
                eng.dma_start(out=o0, in_=i0, **k).then_inc(self.Dm, 16)
            with eng.Else():
                ins = eng.dma_start(out=o1, in_=i1, **k).then_inc(self.Dm, 16)
            self.d += 1
            return ins
        ins = eng.dma_start(out=out, in_=in_, **k)
        self.d += 1
        ins.then_inc(self.Dm, 16)
        return ins

    def fin(self, eng):
        eng.wait_ge(self.C, self.c)
        eng.wait_ge(self.Dm, 16 * self.d)

    def set_fence(self, eng):
        ins = self.nc.gpsimd.engine_nop()
        self.c += 1
        ins.then_inc(self.C, 1)

    def floor_fence(self):
        pass

    force_serial = False


class Pr2(Pr):

    def __init__(self, nc, ctx):
        self.nc = nc
        self.ctx = ctx
        self.c = 0
        self.d = 0
        self.nn = 0
        self.sems = {}
        lane_names = ["ds0", "ds1", "ds2", "ds3", "dg0", "dg1", "dg2", "dg3"]
        for name in ["pe", "dv", "sc", "gp"] + lane_names:
            self.sems[name] = ctx.enter_context(nc.semaphore("S" + name))
        self.cnt = {k: 0 for k in self.sems}
        self.lane_pool = {"sync": ["ds0", "ds1", "ds3"],
                          "gp": ["dg0", "dg1", "dg2"]}
        self.bulk_lane = {"sync": "ds2", "gp": "dg3"}
        self.lane_rr = {"sync": 0, "gp": 0}
        self.hw_rr = 0
        self.slot_regs = {}
        self.waited = {k: {} for k in ("pe", "dv", "sc", "gp", "sync")}
        self.acc = {}
        self.fence_tok = None
        self.floor = {}
        self.force_serial = False

    @staticmethod
    def _ename(eng):
        v = eng.engine.value if hasattr(eng, "engine") else ""
        return {"PE": "pe", "DVE": "dv", "Activation": "sc", "Pool": "gp",
                "SP": "sync"}.get(v, "sync")

    @staticmethod
    def _span(ap):
        off = ap.offset
        if not isinstance(off, int):
            return None
        hi = off
        for st, cnt_ in ap.ap:
            hi += st * (cnt_ - 1)
        return (off, hi + 1)

    def _apinfo(self, ap):
        t = ap.tensor
        name = t.name
        is_psum = "PSUM" in str(getattr(t, "memory_space", "")) or hasattr(t, "bank")
        try:
            from concourse.bass_types import PSumTensorHandle
            is_psum = isinstance(t, PSumTensorHandle)
        except Exception:
            pass
        sp = None if is_psum else self._span(ap)
        if sp is None:
            sp = (0, 1 << 60)
        return name, sp

    def _deps_for(self, reads, writes):
        if self.force_serial:
            return {s: v for s, v in self.cnt.items() if v > 0}
        deps = {}
        def add(tok):
            s, v = tok
            if deps.get(s, 0) < v:
                deps[s] = v
        for ap in reads:
            name, (lo, hi) = self._apinfo(ap)
            for kind, l2, h2, tok in self.acc.get(name, ()):
                if kind == "w" and l2 < hi and lo < h2:
                    add(tok)
        for ap in writes:
            name, (lo, hi) = self._apinfo(ap)
            for kind, l2, h2, tok in self.acc.get(name, ()):
                if l2 < hi and lo < h2:
                    add(tok)
        if self.fence_tok is not None:
            add(self.fence_tok)
        for s, v in self.floor.items():
            if deps.get(s, 0) < v:
                deps[s] = v
        return deps

    def _wait(self, eng, deps):
        en = self._ename(eng)
        wm = self.waited[en]
        for s, v in sorted(deps.items()):
            if en == "pe" and s == "pe":
                continue
            if wm.get(s, 0) >= v:
                continue
            eng.wait_ge(self.sems[s], v)
            wm[s] = v

    def _record(self, reads, writes, tok):
        for kind, aps in (("r", reads), ("w", writes)):
            for ap in aps:
                name, (lo, hi) = self._apinfo(ap)
                lst = self.acc.setdefault(name, [])
                lst.append((kind, lo, hi, tok))
                if len(lst) > 96:
                    old, new = lst[:48], lst[48:]
                    mx = {}
                    anyw = False
                    for k2, l2, h2, (s2, v2) in old:
                        anyw = anyw or (k2 == "w")
                        if mx.get(s2, 0) < v2:
                            mx[s2] = v2
                    merged = [("w" if anyw else "r", 0, 1 << 60, (s2, v2))
                              for s2, v2 in mx.items()]
                    self.acc[name] = merged + new

    def _bump(self, en, n=1):
        self.cnt[en] += n
        return (en, self.cnt[en])

    @staticmethod
    def _extract_aps(args, kwargs=()):
        out = []
        for a in args:
            if isinstance(a, bass.AP):
                out.append(a)
            elif isinstance(a, tuple):
                out.extend(x for x in a if isinstance(x, bass.AP))
        return out

    def op(self, eng, fn, *a, **k):
        aps = self._extract_aps(a) + self._extract_aps(list(k.values()))
        writes, reads = aps[:1], aps[1:]
        deps = self._deps_for(reads, writes)
        self._wait(eng, deps)
        ins = fn(*a, **k)
        en = self._ename(eng)
        tok = self._bump(en)
        ins.then_inc(self.sems[en], 1)
        self._record(reads, writes, tok)
        return ins

    def mm(self, out, lhsT, rhs, **k):
        t = self.nc.tensor
        reads, writes = [lhsT, rhs], [out]
        deps = self._deps_for(reads, writes)
        self._wait(t, deps)
        ins = t.matmul(out, lhsT, rhs, **k)
        tok = self._bump("pe")
        ins.then_inc(self.sems["pe"], 1)
        self._record(reads, writes, tok)
        return ins

    def mmgrp(self, calls):
        t = self.nc.tensor
        reads = [l for _, l, r in calls] + [r for _, l, r in calls]
        writes = [o for o, _, _ in calls]
        deps = self._deps_for(reads, writes)
        self._wait(t, deps)
        n = len(calls)
        for i, (o, l, r) in enumerate(calls):
            ins = t.matmul(o, l, r, start=(i == 0), stop=(i == n - 1))
        tok = self._bump("pe")
        ins.then_inc(self.sems["pe"], 1)
        self._record(reads, writes, tok)

    def tr(self, out_psum, in_sbuf, ident):
        t = self.nc.tensor
        reads, writes = [in_sbuf, ident], [out_psum]
        deps = self._deps_for(reads, writes)
        self._wait(t, deps)
        ins = t.transpose(out_psum, in_sbuf, ident)
        tok = self._bump("pe")
        ins.then_inc(self.sems["pe"], 1)
        self._record(reads, writes, tok)
        return ins

    def dma(self, eng, out, in_, **k):
        bulk = k.pop("bulk", False)
        outs0 = out[0] if isinstance(out, tuple) else out
        ins0 = in_[0] if isinstance(in_, tuple) else in_
        cast = getattr(outs0, "dtype", None) != getattr(ins0, "dtype", None)
        accum = "accum_op" in k
        if cast or accum:
            eng, pool = self.nc.gpsimd, "gp"
        elif eng is self.nc.gpsimd:
            eng, pool = self.nc.gpsimd, "gp"
        else:
            eng, pool = self.nc.sync, "sync"
        en = self._ename(eng)
        if bulk:
            lane = self.bulk_lane[pool]
        else:
            lanes = self.lane_pool[pool]
            lane = lanes[self.lane_rr[pool] % len(lanes)]
            self.lane_rr[pool] += 1
        outs = list(out) if isinstance(out, tuple) else [out]
        ins_ = list(in_) if isinstance(in_, tuple) else [in_]
        reads = list(ins_)
        if "accum_op" in k:
            reads = reads + outs
        deps = self._deps_for(reads, outs)
        if self.cnt[lane] > 0:
            deps[lane] = max(deps.get(lane, 0), self.cnt[lane])
        self._wait(eng, deps)
        if isinstance(out, tuple) or isinstance(in_, tuple):
            o0, o1 = out if isinstance(out, tuple) else (out, out)
            i0, i1 = in_ if isinstance(in_, tuple) else (in_, in_)
            with eng.If_eq(self.slot_regs[en], 0):
                eng.dma_start(out=o0, in_=i0, **k).then_inc(self.sems[lane], 16)
            with eng.Else():
                inst = eng.dma_start(out=o1, in_=i1, **k).then_inc(self.sems[lane], 16)
        else:
            inst = eng.dma_start(out=out, in_=in_, **k)
            inst.then_inc(self.sems[lane], 16)
        tok = self._bump(lane, 16)
        self._record(reads, outs, tok)
        return inst

    def _w(self, eng):
        en = self._ename(eng)
        wm = self.waited[en]
        for s, v in self.cnt.items():
            if v > 0 and wm.get(s, 0) < v:
                eng.wait_ge(self.sems[s], v)
                wm[s] = v

    def set_fence(self, eng_nop_engine):
        GP = self.nc.gpsimd
        ins = GP.engine_nop()
        tok = self._bump("gp")
        ins.then_inc(self.sems["gp"], 1)
        self.fence_tok = tok

    def floor_fence(self):
        for s, v in self.cnt.items():
            if v > 0 and self.floor.get(s, 0) < v:
                self.floor[s] = v

    def fin(self, eng):
        self._w(eng)


def pipeline(mk_gen, n, width):
    act = []
    nxt = 0
    while act or nxt < n:
        while len(act) < width and nxt < n:
            act.append(mk_gen(nxt))
            nxt += 1
        for g in list(act):
            try:
                next(g)
            except StopIteration:
                act.remove(g)


def _consts(nc, p):
    cs = {}
    eye = np.eye(128, dtype=np.float32)
    cs["idf"] = nc.inline_tensor(eye, name="idf")
    cs["idb"] = nc.inline_tensor(eye.astype(NBF), name="idb")
    cs["msl"] = nc.inline_tensor(np.tril(np.ones((128, 128), np.float32), -1), name="msl")
    cs["mui"] = nc.inline_tensor(np.triu(np.ones((128, 128), np.float32)), name="mui")
    cs["onesf"] = nc.inline_tensor(np.ones((1, 128), np.float32), name="onesf")
    cs["onescol_f"] = nc.inline_tensor(np.ones((128, 1), np.float32), name="onescol_f")
    t = {}
    for k, v in cs.items():
        dt = BF16 if k == "idb" else F32
        sbuf = p.sb("c_" + k, list(v.shape), dt)
        p.dma(nc.sync, sbuf[:], v.ap())
        t[k] = sbuf
    return t



def emit_l1(nc, p, C, io):
    V, SC, GP, SY = nc.vector, nc.scalar, nc.gpsimd, nc.sync
    with p.scope():
        wq_s = p.sb("wq_s", [128, 8 * 512], BF16)
        wk_s = p.sb("wk_s", [128, 8 * 512], BF16)
        wv_s = p.sb("wv_s", [128, 8 * 512], BF16)
        for dst, src in ((wq_s, io["wq"]), (wk_s, io["wk"]), (wv_s, io["wv"])):
            p.dma(GP, dst[:].rearrange("p (c n) -> p c n", c=8),
                  src.rearrange("(c p) n -> p c n", p=128))
        wbg_s = p.sb("wbg_s", [128, 8 * 4], BF16)
        p.dma(GP, wbg_s[:].rearrange("p (c n) -> p c n", c=8),
              io["wbg"].rearrange("(c p) n -> p c n", p=128))
        wmx_s = p.sb("wmx_s", [128, 8 * 2], BF16)
        p.dma(GP, wmx_s[:].rearrange("p (c n) -> p c n", c=8),
              io["wmx"].rearrange("(c p) n -> p c n", p=128))
        cq_s = p.sb("cq_s", [128, 4 * 4])
        ck_s = p.sb("ck_s", [128, 4 * 4])
        cv_s = p.sb("cv_s", [128, 4 * 4])
        for dst, src in ((cq_s, io["cq"]), (ck_s, io["ck"]), (cv_s, io["cv"])):
            p.dma(SY, dst[:].rearrange("p (c n) -> p c n", c=4),
                  src.rearrange("(c p) n -> p c n", p=128))
        cmx_s = p.sb("cmx_s", [2, 4])
        p.dma(SY, cmx_s[:], io["cmx"])
        mb_s = p.sb("mb_s", [2, 1])
        p.dma(SY, mb_s[:], io["mbias"])
        nmb_s = p.sb("nmb_s", [2, 1])
        rw_s = p.sb("rw_s", [128, 2])
        p.dma(SY, rw_s[:], io["rmsw"])

        epsa = p.sb("epsa", [128, 2])
        xt = p.sb("xt", [128, D])
        xc = p.sb("xc", [128, D])
        sqb = p.sb("sqb", [128, D])
        stats = p.sb("stats", [128, 8])
        xlb = p.sb("xlb", [128, D], BF16)
        xfm = p.sb("xfm", [128, 8 * 512], BF16)
        qfm = p.sb("qfm", [128, 4 * 512], BF16)
        kfm = p.sb("kfm", [128, 4 * 512], BF16)
        vfm = p.sb("vfm", [128, 4 * 512], BF16)
        pad = p.sb("pad", [128, 515])
        cvt = p.sb("cvt", [128, 512])
        cvt2 = p.sb("cvt2", [128, 512])
        tmpf = p.sb("tmpf", [128, 512])
        carry = p.sb("carry", [128, 3 * 12])
        mpad = p.sb("mpad", [2, 515])
        mcar = p.sb("mcar", [2, 3])
        mixf = p.sb("mixf", [2, 512])
        mixe = p.sb("mixe", [2, 512])
        mixt = p.sb("mixt", [128, 2 * 4])
        bgt = p.sb("bgt", [128, 4 * 4])
        sgt = p.sb("sgt", [128, 8])
        nbgt = p.sb("nbgt", [128, 4 * 4])
        alp = p.sb("alp", [128, 8])
        kgm = p.sb("kgm", [128, 8])
        bkg = p.sb("bkg", [128, 8])
        sqf = p.sb("sqf", [128, 512])
        khat_p = [p.sb("khat", [128, 256], BF16) for _ in range(2)]
        bkh_p = [p.sb("bkh", [128, 256], BF16) for _ in range(2)]
        bvt_p = [p.sb("bvt", [128, 256], BF16) for _ in range(2)]
        khf_p = [p.sb("khf", [128, 256], BF16) for _ in range(2)]
        nsb_p = [p.sb("nsb", [128, 128], BF16) for _ in range(2)]
        ntb_p = [p.sb("ntb", [128, 128], BF16) for _ in range(2)]
        p2a_p = [p.sb("p2a", [128, 128], BF16) for _ in range(2)]
        p2b_p = [p.sb("p2b", [128, 128], BF16) for _ in range(2)]
        pt2a_p = [p.sb("pt2a", [128, 128], BF16) for _ in range(2)]
        pt2b_p = [p.sb("pt2b", [128, 128], BF16) for _ in range(2)]
        ata_p = [p.sb("ata", [128, 128], BF16) for _ in range(2)]
        atb_p = [p.sb("atb", [128, 128], BF16) for _ in range(2)]
        ghat_p = [p.sb("ghat", [128, 128], BF16) for _ in range(2)]
        usb_p = [p.sb("usb", [128, 256]) for _ in range(2)]
        wfm_p = [p.sb("wfm", [128, 256], BF16) for _ in range(2)]
        etm_p = [p.sb("etm", [128, 256]) for _ in range(2)]
        S_f = p.sb("S_f", [128, 2 * 2 * 256])
        S_b = p.sb("S_b", [128, 2 * 2 * 256], BF16)
        gbc = p.sb("gbc", [128, 512])
        grow = p.sb("grow", [1, 512])
        rsc = p.sb("rsc", [128, 512])
        esc = p.sb("esc", [128, 2 * 2 * 512])
        ecar = p.sb("ecar", [128, 4])
        t0b_p = [p.sb("t0b", [128, 256]) for _ in range(2)]
        dsb_p = [p.sb("dsb", [128, 256]) for _ in range(2)]
        obb_p = [p.sb("obb", [128, 256]) for _ in range(2)]
        orb_p = [p.sb("orb", [128, 256], BF16) for _ in range(2)]
        uadj_p = [p.sb("uadj", [128, 256], BF16) for _ in range(2)]
        sqs_p = [p.sb("sqs", [128, 256]) for _ in range(2)]
        sts_p = [p.sb("sts", [128, 4]) for _ in range(2)]
        omf = p.sb("omf", [128, 4 * 512], BF16)

        psA = p.ps("psA", [128, 512])
        psB = p.ps("psB", [128, 512])
        psG_p = [p.ps("psG", [128, 512]) for _ in range(2)]
        psO = p.ps("psO", [128, 512])
        psS = p.ps("psS", [128, 512])
        psT_p = [p.ps("psT", [128, 1024], BF16) for _ in range(2)]
        psT = psT_p[0]

        p.op(V, V.memset, epsa[:, 0:1], 1e-5)
        p.op(V, V.memset, epsa[:, 1:2], 1e-6)
        p.op(V, V.tensor_scalar, nmb_s[:], mb_s[:], -1.0, None, ALU.mult)
        p.op(V, V.memset, carry[:], 0.0)
        p.op(V, V.memset, mcar[:], 0.0)
        p.op(V, V.memset, S_f[:], 0.0)
        p.op(V, V.memset, S_b[:], 0.0)
        p.op(V, V.memset, ecar[:], 0.0)

        import os
        sm1 = os.environ.get("SER1MASK", "000000")
        base_serial = p.force_serial
        def sser(j):
            p.force_serial = base_serial or sm1[j] == "1"
        NG = LSEQ // 512
        for g in range(NG):
            t0 = g * 512
            sser(0)
            for i4 in range(4):
                tt = t0 + i4 * 128
                p.dma(SY, xt[:], io["x"][tt:tt + 128, :])
                p.op(V, V.reduce_sum, stats[:, 0:1], xt[:], AX.X)
                p.op(V, V.tensor_scalar, stats[:, 1:2], stats[:, 0:1], 1.0 / D, None, ALU.mult)
                p.op(V, V.tensor_scalar, xc[:], xt[:], stats[:, 1:2], None, ALU.subtract)
                p.op(V, V.tensor_mul, sqb[:], xc[:], xc[:])
                p.op(V, V.reduce_sum, stats[:, 2:3], sqb[:], AX.X)
                p.op(SC, SC.activation, stats[:, 3:4], stats[:, 2:3], AF.Ln, epsa[:, 0:1], 1.0 / D)
                p.op(SC, SC.activation, stats[:, 4:5], stats[:, 3:4], AF.Exp, scale=-0.5)
                p.op(V, V.tensor_scalar, xlb[:], xc[:], stats[:, 4:5], None, ALU.mult)
                for c in range(8):
                    p.tr(psT[:, c * 128:(c + 1) * 128], xlb[:, c * 128:(c + 1) * 128], C["idb"][:])
                src = psT[:, 0:1024].rearrange("p (c t) -> p c t", c=8)
                dst = xfm[:].rearrange("p (c t) -> p c t", c=8)[:, :, i4 * 128:(i4 + 1) * 128]
                p.op(V, V.tensor_copy, dst, src)
                p.mmgrp([(psS[:, 0:4], xfm[:, c * 512 + i4 * 128: c * 512 + (i4 + 1) * 128],
                          wbg_s[:, c * 4:(c + 1) * 4]) for c in range(8)])
                p.op(SC, SC.activation, sgt[:, 0:4], psS[:, 0:4], AF.Exp, scale=-1.0)
                p.op(V, V.tensor_scalar, sgt[:, 4:8], sgt[:, 0:4], 1.0, None, ALU.add)
                p.op(V, V.reciprocal, bgt[:, i4 * 4:(i4 + 1) * 4], sgt[:, 4:8])
            p.op(V, V.tensor_scalar, nbgt[:], bgt[:], -1.0, None, ALU.mult)

            sser(1)
            p.mmgrp([(psS[0:2, 0:512], wmx_s[:, c * 2:(c + 1) * 2],
                      xfm[:, c * 512:(c + 1) * 512]) for c in range(8)])
            p.op(V, V.tensor_copy, mpad[0:2, 0:3], mcar[:])
            p.op(V, V.tensor_copy, mpad[0:2, 3:515], psS[0:2, 0:512])
            p.op(V, V.tensor_copy, mcar[:], mpad[0:2, 512:515])
            p.op(V, V.tensor_scalar, mixf[:], mpad[0:2, 0:512], cmx_s[0:2, 0:1], None, ALU.mult)
            for j in range(1, 4):
                p.op(V, V.scalar_tensor_tensor, mixf[:], mpad[0:2, j:j + 512],
                     cmx_s[0:2, j:j + 1], mixf[:], ALU.mult, ALU.add)
            p.op(SC, SC.activation, tmpf[0:2, 0:512], mixf[:], AF.Silu)
            p.op(SC, SC.activation, mixe[:], tmpf[0:2, 0:512], AF.Exp, nmb_s[0:2, 0:1], -1.0)
            p.op(V, V.tensor_scalar, mixe[:], mixe[:], 1.0, None, ALU.add)
            p.op(V, V.reciprocal, mixf[:], mixe[:])
            for i4 in range(4):
                p.tr(psS[:, 504:506], mixf[0:2, i4 * 128:(i4 + 1) * 128], C["idf"][0:2, 0:2])
                p.op(V, V.tensor_copy, mixt[:, i4 * 2:(i4 + 1) * 2], psS[:, 504:506])

            sser(2)
            for si, (wsb, csb, outb) in enumerate(((wq_s, cq_s, qfm), (wk_s, ck_s, kfm), (wv_s, cv_s, vfm))):
                for c4 in range(4):
                    p.mmgrp([(psA[:, 0:512],
                              wsb[:, cin * 512 + c4 * 128: cin * 512 + (c4 + 1) * 128],
                              xfm[:, cin * 512:(cin + 1) * 512]) for cin in range(8)])
                    cslot = si * 4 + c4
                    p.op(V, V.tensor_copy, pad[:, 0:3], carry[:, cslot * 3:(cslot + 1) * 3])
                    p.op(V, V.tensor_copy, pad[:, 3:515], psA[:, 0:512])
                    p.op(V, V.tensor_copy, carry[:, cslot * 3:(cslot + 1) * 3], pad[:, 512:515])
                    p.op(V, V.tensor_scalar, cvt[:], pad[:, 0:512], csb[:, c4 * 4:c4 * 4 + 1], None, ALU.mult)
                    for j in range(1, 4):
                        p.op(V, V.scalar_tensor_tensor, cvt[:], pad[:, j:j + 512],
                             csb[:, c4 * 4 + j:c4 * 4 + j + 1], cvt[:], ALU.mult, ALU.add)
                    if si == 2:
                        p.op(SC, SC.activation, outb[:, c4 * 512:(c4 + 1) * 512], cvt[:], AF.Silu)
                    else:
                        p.op(SC, SC.activation, tmpf[:], cvt[:], AF.Silu)
                        p.op(SC, SC.activation, outb[:, c4 * 512:(c4 + 1) * 512], tmpf[:], AF.Silu)

            sser(3)
            for (fmb, dstb) in ((qfm, alp), (kfm, kgm)):
                for h in range(2):
                    for i4 in range(4):
                        p.op(SC, SC.activation, sqf[:, 0:128],
                             fmb[:, (2 * h) * 512 + i4 * 128:(2 * h) * 512 + (i4 + 1) * 128], AF.Square)
                        p.op(SC, SC.activation, sqf[:, 128:256],
                             fmb[:, (2 * h + 1) * 512 + i4 * 128:(2 * h + 1) * 512 + (i4 + 1) * 128], AF.Square)
                        p.mmgrp([(psS[:, 8 + i4 * 2 + h: 9 + i4 * 2 + h],
                                  sqf[:, d * 128:(d + 1) * 128], C["onescol_f"][:]) for d in range(2)])
                p.op(SC, SC.activation, stats[:, 0:8], psS[:, 8:16], AF.Ln, epsa[:, 1:2])
                p.op(SC, SC.activation, dstb[:], stats[:, 0:8], AF.Exp, scale=-0.5)
            for col in range(8):
                i4, h = col // 2, col % 2
                p.op(V, V.tensor_mul, bkg[:, col:col + 1], kgm[:, col:col + 1],
                     bgt[:, i4 * 4 + h:i4 * 4 + h + 1])

            sser(4)
            for h in range(2):
                for i4 in range(4):
                    p.tr(psS[0:1, 256:384], bgt[:, i4 * 4 + 2 + h:i4 * 4 + 3 + h], C["idf"][:])
                    p.op(V, V.tensor_copy, grow[0:1, i4 * 128:(i4 + 1) * 128], psS[0:1, 256:384])
                p.mm(psB[:, 0:512], C["onesf"][:], grow[:], start=True, stop=True)
                p.op(V, V.tensor_copy, gbc[:], psB[:, 0:512])
                for dch in range(2):
                    vsl = vfm[:, (2 * h + dch) * 512:(2 * h + dch + 1) * 512]
                    p.op(V, V.tensor_mul, rsc[:], gbc[:], vsl)
                    p.op(V, V.tensor_sub, rsc[:], vsl, rsc[:])
                    eslot = (h * 2 + dch)
                    p.op(V, V.tensor_tensor_scan, esc[:, eslot * 512:(eslot + 1) * 512],
                         gbc[:], rsc[:], ecar[:, eslot:eslot + 1], ALU.mult, ALU.add)
                    p.op(V, V.tensor_copy, ecar[:, eslot:eslot + 1],
                         esc[:, eslot * 512 + 511:eslot * 512 + 512])

            sser(5)
            def pre_chunk(i4, h):
                cs = (i4 * 2 + h) % 2
                psT = psT_p[cs]
                psG = psG_p[cs]
                kgcol = kgm[:, i4 * 2 + h:i4 * 2 + h + 1]
                bkgcol = bkg[:, i4 * 2 + h:i4 * 2 + h + 1]
                bcol = bgt[:, i4 * 4 + h:i4 * 4 + h + 1]
                nbcol = nbgt[:, i4 * 4 + h:i4 * 4 + h + 1]
                ksl = lambda d: kfm[:, (2 * h + d) * 512 + i4 * 128:(2 * h + d) * 512 + (i4 + 1) * 128]
                vsl = lambda d: vfm[:, (2 * h + d) * 512 + i4 * 128:(2 * h + d) * 512 + (i4 + 1) * 128]
                qsl = lambda d: qfm[:, (2 * h + d) * 512 + i4 * 128:(2 * h + d) * 512 + (i4 + 1) * 128]
                khat, bkh, bvt, khf = khat_p[cs], bkh_p[cs], bvt_p[cs], khf_p[cs]
                nsb, ntb, ghat = nsb_p[cs], ntb_p[cs], ghat_p[cs]
                p2a, p2b, pt2a, pt2b = p2a_p[cs], p2b_p[cs], pt2a_p[cs], pt2b_p[cs]
                ata, atb = ata_p[cs], atb_p[cs]
                for d in range(2):
                    p.tr(psT[:, d * 128:(d + 1) * 128], ksl(d), C["idb"][:])
                yield
                p.op(SC, SC.activation, khat[:], psT[:, 0:256], AF.Identity, scale=kgcol)
                yield
                p.op(SC, SC.activation, bkh[:], psT[:, 0:256], AF.Identity, scale=bkgcol)
                yield
                for d in range(2):
                    p.tr(psT[:, 256 + d * 128:256 + (d + 1) * 128], vsl(d), C["idb"][:])
                yield
                p.op(SC, SC.activation, bvt[:], psT[:, 256:512], AF.Identity, scale=bcol)
                yield
                for d in range(2):
                    p.tr(psT[:, 512 + d * 128:512 + (d + 1) * 128],
                         khat[:, d * 128:(d + 1) * 128], C["idb"][:])
                yield
                p.op(SC, SC.activation, khf[:], psT[:, 512:768], AF.Identity)
                yield
                p.mmgrp([(psG[:, 0:128], khf[:, d * 128:(d + 1) * 128],
                          khf[:, d * 128:(d + 1) * 128]) for d in range(2)])
                yield
                p.op(V, V.scalar_tensor_tensor, nsb[:], psG[:, 0:128], nbcol,
                     C["msl"][:], ALU.mult, ALU.mult)
                yield
                p.tr(psT[:, 768:896], nsb[:], C["idb"][:])
                yield
                p.op(SC, SC.activation, ntb[:], psT[:, 768:896], AF.Identity)
                yield
                p.op(V, V.tensor_add, ata[:], ntb[:], C["idf"][:])
                yield
                Pc, Ptc, Atc = nsb, ntb, ata
                Pn, Ptn, Atn = p2a, pt2a, atb
                e = 2
                tog = 0
                while e <= 64:
                    p.mm(psG[:, 0:128], Ptc[:], Pc[:], start=True, stop=True)
                    yield
                    p.op(SC, SC.activation, Pn[:], psG[:, 0:128], AF.Identity)
                    yield
                    if e < 64:
                        p.mm(psG[:, 128:256], Pc[:], Ptc[:], start=True, stop=True)
                        yield
                        p.op(V, V.tensor_copy, Ptn[:], psG[:, 128:256])
                        yield
                    p.mmgrp([(psG[:, 256:384], Pn[:], Atc[:]),
                             (psG[:, 256:384], C["idb"][:], Atc[:])])
                    yield
                    if tog == 0:
                        p.op(V, V.tensor_copy, Atn[:], psG[:, 256:384])
                    else:
                        p.op(SC, SC.activation, Atn[:], psG[:, 256:384], AF.Identity)
                    yield
                    tog ^= 1
                    Pc, Pn = Pn, (p2b if Pn is p2a else p2a)
                    Ptc, Ptn = Ptn, (pt2b if Ptn is pt2a else pt2a)
                    Atc, Atn = Atn, (ata if Atn is atb else atb)
                    e *= 2
                At = Atc
                p.mm(psA[:, 0:256], At[:], bvt[:], start=True, stop=True)
                yield
                p.op(V, V.tensor_copy, usb_p[cs][:], psA[:, 0:256])
                yield
                for d in range(2):
                    p.mm(psA[:, 256 + d * 128:256 + (d + 1) * 128],
                         bkh[:, d * 128:(d + 1) * 128], At[:], start=True, stop=True)
                yield
                p.op(SC, SC.activation, wfm_p[cs][:], psA[:, 256:512], AF.Identity)
                yield
                p.mmgrp([(psG[:, 0:128], khf[:, d * 128:(d + 1) * 128], qsl(d))
                         for d in range(2)])
                yield
                p.op(V, V.tensor_mul, ghat[:], psG[:, 0:128], C["mui"][:])
                yield
                for d in range(2):
                    p.tr(psS[:, 256 + d * 128:256 + (d + 1) * 128],
                         esc[:, (h * 2 + d) * 512 + i4 * 128:(h * 2 + d) * 512 + (i4 + 1) * 128],
                         C["idf"][:])
                yield
                p.op(V, V.tensor_copy, etm_p[cs][:], psS[:, 256:512])
                yield

            def scan_chunk(i4, h):
                cs = (i4 * 2 + h) % 2
                psT = psT_p[cs]
                acol = alp[:, i4 * 2 + h:i4 * 2 + h + 1]
                mcol = mixt[:, i4 * 2 + h:i4 * 2 + h + 1]
                uadj, t0b, dsb = uadj_p[cs], t0b_p[cs], dsb_p[cs]
                obb, orb, sqs, sts = obb_p[cs], orb_p[cs], sqs_p[cs], sts_p[cs]
                Ssl = lambda d: S_f[:, (h * 2 + d) * 256:(h * 2 + d + 1) * 256]
                Sbl = lambda d: S_b[:, (h * 2 + d) * 256:(h * 2 + d + 1) * 256]
                qsl = lambda d: qfm[:, (2 * h + d) * 512 + i4 * 128:(2 * h + d) * 512 + (i4 + 1) * 128]
                khat, khf, ghat = khat_p[cs], khf_p[cs], ghat_p[cs]
                p.mmgrp([(psB[:, 0:256], wfm_p[cs][:, d * 128:(d + 1) * 128], Sbl(d))
                         for d in range(2)])
                yield
                p.op(V, V.tensor_sub, uadj[:], usb_p[cs][:], psB[:, 0:256])
                yield
                o_calls = [(psO[:, 0:256], qsl(d), Sbl(d)) for d in range(2)]
                o_calls.append((psO[:, 0:256], ghat[:], uadj[:]))
                p.mmgrp(o_calls)
                yield
                for d in range(2):
                    p.mm(psB[:, d * 256:(d + 1) * 256], khat[:, d * 128:(d + 1) * 128],
                         uadj[:], start=True, stop=True)
                yield
                for d in range(2):
                    p.op(V, V.tensor_add, Ssl(d), Ssl(d), psB[:, d * 256:(d + 1) * 256])
                    yield
                    p.op(V, V.tensor_copy, Sbl(d), Ssl(d))
                    yield
                p.op(V, V.tensor_scalar, t0b[:], psO[:, 0:256], acol, None, ALU.mult)
                yield
                p.op(V, V.tensor_sub, dsb[:], etm_p[cs][:], t0b[:])
                yield
                p.op(V, V.scalar_tensor_tensor, obb[:], dsb[:], mcol, t0b[:], ALU.mult, ALU.add)
                yield
                p.op(V, V.tensor_mul, sqs[:], obb[:], obb[:])
                yield
                p.op(V, V.reduce_sum, sts[:, 0:1], sqs[:], AX.X)
                yield
                p.op(SC, SC.activation, sts[:, 1:2], sts[:, 0:1], AF.Ln, epsa[:, 0:1], 1.0 / 256)
                yield
                p.op(SC, SC.activation, sts[:, 2:3], sts[:, 1:2], AF.Exp, scale=-0.5)
                yield
                p.op(V, V.tensor_scalar, orb[:], obb[:], sts[:, 2:3], None, ALU.mult)
                yield
                for d in range(2):
                    p.tr(psT[:, d * 128:(d + 1) * 128],
                         orb[:, d * 128:(d + 1) * 128], C["idb"][:])
                    yield
                    p.op(SC, SC.activation,
                         omf[:, (2 * h + d) * 512 + i4 * 128:(2 * h + d) * 512 + (i4 + 1) * 128],
                         psT[:, d * 128:(d + 1) * 128], AF.Identity, scale=rw_s[:, d:d + 1])
                    yield

            def interleave(*gens):
                act = [g for g in gens if g is not None]
                while act:
                    for g in list(act):
                        try:
                            next(g)
                        except StopIteration:
                            act.remove(g)

            chunks = [(i4, h) for i4 in range(4) for h in range(2)]
            interleave(pre_chunk(*chunks[0]))
            for ci in range(len(chunks)):
                nxt = pre_chunk(*chunks[ci + 1]) if ci + 1 < len(chunks) else None
                interleave(scan_chunk(*chunks[ci]), nxt)
            p.force_serial = base_serial
            for c4 in range(4):
                p.dma(GP, io["om_out"](c4, t0), omf[:, c4 * 512:(c4 + 1) * 512])



def emit_l2(nc, p, C, io):
    V, SC, GP, SY = nc.vector, nc.scalar, nc.gpsimd, nc.sync
    NT = T // 128
    with p.scope():
        wo_s = p.sb("wo_s", [128, 8 * 1024], BF16)
        p.dma(GP, wo_s[:].rearrange("p (c n) -> p c n", c=8),
              io["wo"].rearrange("(c p) n -> p c n", p=128))
        wr_s = p.sb("wr_s", [128, 8 * 8])
        p.dma(SY, wr_s[:].rearrange("p (c n) -> p c n", c=8),
              io["wr"].rearrange("(c p) n -> p c n", p=128))
        wrh = p.sb("wrh", [128, 8 * 8], BF16)
        wrl = p.sb("wrl", [128, 8 * 8], BF16)
        p.op(V, V.tensor_copy, wrh[:], wr_s[:])
        p.op(V, V.tensor_sub, wrl[:], wr_s[:], wrh[:])
        epsa = p.sb("epsa", [128, 1])
        omf_s = p.sb("omf_s", [128, 8 * 512], BF16)
        xr_t = p.sb("xr_t", [128, D])
        x1_s = p.sb("x1_s", [128, D])
        xc = p.sb("xc", [128, D])
        sqb = p.sb("sqb", [128, D])
        stats = p.sb("stats", [128, 8])
        xlf = p.sb("xlf", [128, D])
        xhi = p.sb("xhi", [128, D], BF16)
        xlo = p.sb("xlo", [128, D], BF16)
        xhfm = p.sb("xhfm", [128, 8 * T], BF16)
        xlofm = p.sb("xlofm", [128, 8 * 128], BF16)
        m1 = p.sb("m1", [128, 1])
        m2 = p.sb("m2", [128, 1])
        msk1 = p.sb("msk1", [128, 8])
        msk2 = p.sb("msk2", [128, 8])
        l2b = p.sb("l2b", [128, 8])
        pmb = p.sb("pmb", [128, 8])
        p1b = p.sb("p1b", [128, 1])
        mgate = p.sb("mgate", [128, NT * 8])
        p.op(V, V.memset, epsa[:], 1e-5)

        ctxA = p.scope()
        ctxA.__enter__()
        psY = p.ps("psY", [128, 1024])
        psT = p.ps("psT", [128, 1024], BF16)
        psF = p.ps("psF", [128, 512])

        for i in range(NT):
            g, i4 = i // 4, i % 4
            if i4 == 0:
                for c in range(8):
                    p.dma(GP, omf_s[:, c * 512:(c + 1) * 512], io["om_in"](c, g))
            p.dma(GP, xr_t[:], io["xr_row"](i))
            for n0 in range(2):
                p.mmgrp([(psY[:, n0 * 512:(n0 + 1) * 512],
                          omf_s[:, c * 512 + i4 * 128:c * 512 + (i4 + 1) * 128],
                          wo_s[:, c * 1024 + n0 * 512:c * 1024 + (n0 + 1) * 512])
                         for c in range(8)])
            p.op(V, V.tensor_add, x1_s[:], psY[:], xr_t[:])
            p.dma(GP, io["x2_row"](i), x1_s[:])
            p.op(V, V.reduce_sum, stats[:, 0:1], x1_s[:], AX.X)
            p.op(V, V.tensor_scalar, stats[:, 1:2], stats[:, 0:1], 1.0 / D, None, ALU.mult)
            p.op(V, V.tensor_scalar, xc[:], x1_s[:], stats[:, 1:2], None, ALU.subtract)
            p.op(V, V.tensor_mul, sqb[:], xc[:], xc[:])
            p.op(V, V.reduce_sum, stats[:, 2:3], sqb[:], AX.X)
            p.op(SC, SC.activation, stats[:, 3:4], stats[:, 2:3], AF.Ln, epsa[:, 0:1], 1.0 / D)
            p.op(SC, SC.activation, stats[:, 4:5], stats[:, 3:4], AF.Exp, scale=-0.5)
            p.op(V, V.tensor_scalar, xlf[:], xc[:], stats[:, 4:5], None, ALU.mult)
            p.op(V, V.tensor_copy, xhi[:], xlf[:])
            p.op(V, V.tensor_sub, xlo[:], xlf[:], xhi[:])
            for c in range(8):
                p.tr(psT[:, c * 128:(c + 1) * 128], xhi[:, c * 128:(c + 1) * 128], C["idb"][:])
            p.op(V, V.tensor_copy,
                 xhfm[:].rearrange("p (c t) -> p c t", c=8)[:, :, i * 128:(i + 1) * 128],
                 psT[:, 0:1024].rearrange("p (c t) -> p c t", c=8))
            for c in range(8):
                p.tr(psT[:, c * 128:(c + 1) * 128], xlo[:, c * 128:(c + 1) * 128], C["idb"][:])
            p.op(V, V.tensor_copy, xlofm[:], psT[:, 0:1024])
            calls = []
            for c in range(8):
                xh = xhfm[:, c * T + i * 128:c * T + (i + 1) * 128]
                xl = xlofm[:, c * 128:(c + 1) * 128]
                calls.append((psF[:, 0:8], xh, wrh[:, c * 8:(c + 1) * 8]))
                calls.append((psF[:, 0:8], xh, wrl[:, c * 8:(c + 1) * 8]))
                calls.append((psF[:, 0:8], xl, wrh[:, c * 8:(c + 1) * 8]))
            p.mmgrp(calls)
            p.op(V, V.reduce_max, m1[:], psF[:, 0:8], AX.X)
            p.op(V, V.tensor_tensor, msk1[:], psF[:, 0:8], m1[:].to_broadcast([128, 8]), ALU.is_equal)
            p.op(V, V.scalar_tensor_tensor, l2b[:], msk1[:], -1e30, psF[:, 0:8], ALU.mult, ALU.add)
            p.op(V, V.reduce_max, m2[:], l2b[:], AX.X)
            p.op(V, V.tensor_tensor, msk2[:], l2b[:], m2[:].to_broadcast([128, 8]), ALU.is_equal)
            p.op(V, V.tensor_sub, stats[:, 5:6], m1[:], m2[:])
            p.op(SC, SC.activation, stats[:, 6:7], stats[:, 5:6], AF.Exp, scale=-1.0)
            p.op(V, V.tensor_scalar, stats[:, 7:8], stats[:, 6:7], 1.0, None, ALU.add)
            p.op(V, V.reciprocal, p1b[:], stats[:, 7:8])
            p.op(V, V.tensor_sub, pmb[:], msk1[:], msk2[:])
            p.op(V, V.scalar_tensor_tensor, mgate[:, i * 8:(i + 1) * 8], pmb[:], p1b[:],
                 msk2[:], ALU.mult, ALU.add)

        ctxA.__exit__(None, None, None)
        p.floor_fence()
        with p.scope():
            w1b = [p.sb("w1b", [128, 8 * 1024], BF16) for _ in range(2)]
            w2b = [p.sb("w2b", [128, 8 * 1024], BF16) for _ in range(2)]
            psHp = [p.ps("psHp", [128, 1024]) for _ in range(2)]
            psYp = [p.ps("psYp", [128, 1024]) for _ in range(2)]
            rsbp = [p.sb("rsbp", [128, 1024]) for _ in range(2)]
            hbp = [p.sb("hbp", [128, 1024], BF16) for _ in range(2)]
            ysbp = [p.sb("ysbp", [128, 1024]) for _ in range(2)]

            def ldw(e):
                q = e % 2
                p.dma(GP, w1b[q][:].rearrange("p (c n) -> p c n", c=8),
                      io["w1e"](e).rearrange("(c p) n -> p c n", p=128), bulk=True)
                p.dma(GP, w2b[q][:].rearrange("p (c n) -> p c n", c=8),
                      io["w2e"](e).rearrange("(c p) n -> p c n", p=128), bulk=True)
            ldw(0)
            for e in range(8):
                if e + 1 < 8:
                    ldw(e + 1)
                w1_s, w2_s = w1b[e % 2], w2b[e % 2]
                for i in range(NT):
                    q = i % 2
                    psH, psY = psHp[q], psYp[q]
                    rsb, hb, ysb = rsbp[q], hbp[q], ysbp[q]
                    for dc in range(8):
                        p.mmgrp([(psH[:, dc * 128:(dc + 1) * 128],
                                  w1_s[:, cin * 1024 + dc * 128:cin * 1024 + (dc + 1) * 128],
                                  xhfm[:, cin * T + i * 128:cin * T + (i + 1) * 128])
                                 for cin in range(8)])
                    p.op(SC, SC.activation, rsb[:], psH[:], AF.Relu)
                    p.op(V, V.tensor_mul, hb[:], rsb[:], rsb[:])
                    for n0 in range(2):
                        p.mmgrp([(psY[:, n0 * 512:(n0 + 1) * 512],
                                  hb[:, dc * 128:(dc + 1) * 128],
                                  w2_s[:, dc * 1024 + n0 * 512:dc * 1024 + (n0 + 1) * 512])
                                 for dc in range(8)])
                    if q == 0:
                        p.op(V, V.tensor_scalar, ysb[:], psY[:],
                             mgate[:, i * 8 + e:i * 8 + e + 1], None, ALU.mult)
                    else:
                        p.op(SC, SC.activation, ysb[:], psY[:], AF.Identity,
                             scale=mgate[:, i * 8 + e:i * 8 + e + 1])
                    p.dma(GP, io["x2_row"](i), ysb[:], accum_op=ALU.add)



def emit_l3(nc, p, C, io):
    V, SC, GP, SY = nc.vector, nc.scalar, nc.gpsimd, nc.sync
    NT = T // 128
    with p.scope():
        wqT = p.sb("wqT", [128, 8 * 1024], BF16)
        wkT = p.sb("wkT", [128, 8 * 1024], BF16)
        wvT = p.sb("wvT", [128, 8 * 1024], BF16)
        for scr, w in ((io["scrq"], io["wq1"]), (io["scrk"], io["wk1"]), (io["scrv"], io["wv1"])):
            p.dma(GP, scr.ap(), w)
        for dst, scr in ((wqT, io["scrq"]), (wkT, io["scrk"]), (wvT, io["scrv"])):
            for c in range(8):
                p.dma(SY, dst[:, c * 1024:(c + 1) * 1024],
                      scr.ap()[0:1024, c * 128:(c + 1) * 128], transpose=True)
        bq_s = p.sb("bq_s", [128, 8])
        bk_s = p.sb("bk_s", [128, 8])
        p.dma(SY, bq_s[:], io["bq"])
        p.dma(SY, bk_s[:], io["bk"])
        prj = p.sb("prj", [128, 128], BF16)
        prf = p.sb("prf", [64, 128])
        p.dma(SY, prf[:], io["proj"])
        p.op(V, V.tensor_scalar, prj[0:64, :], prf[:], 1.0 / (8.0 * TWO_PI), None, ALU.mult)
        p.op(V, V.tensor_scalar, prj[64:128, :], prf[:], 1.0 / (8.0 * TWO_PI), None, ALU.mult)

        epsa = p.sb("epsa", [128, 2])
        pih = p.sb("pih", [128, 1])
        mgc = p.sb("mgc", [128, 1])
        xt_p = [p.sb("xt", [128, D]) for _ in range(2)]
        xc_p = [p.sb("xc", [128, D]) for _ in range(2)]
        sqb_p = [p.sb("sqb", [128, D]) for _ in range(2)]
        stats_p = [p.sb("stats", [128, 8]) for _ in range(2)]
        xlb_p = [p.sb("xlb", [128, D], BF16) for _ in range(2)]
        xfm_p = [p.sb("xfm", [128, 8 * 128], BF16) for _ in range(2)]
        qf_g_p = [p.sb("qf_g", [128, 8 * 512], BF16) for _ in range(2)]
        kf_p = [p.sb("kf_s", [128, 8 * 128], BF16) for _ in range(2)]
        vt_p = [p.sb("vt_s", [128, 1024], BF16) for _ in range(2)]
        rmag_p = [p.sb("rmag", [128, 128]) for _ in range(2)]
        rint_p = [p.sb("rint", [128, 128]) for _ in range(2)]
        fsb_p = [p.sb("fsb", [128, 128]) for _ in range(2)]
        fab_p = [p.sb("fab", [128, 128]) for _ in range(2)]
        kp_p = [p.sb("kp_s", [128, 256], BF16) for _ in range(2)]
        onescol_b = p.sb("onescol_b", [128, 1], BF16)
        kvout = p.sb("kvout", [128, 32 * 64])
        skout = p.sb("skout", [128, 32])

        psKV = [p.ps(f"psKV{j}", [128, 512]) for j in range(4)]
        psSK = p.ps("psSK", [128, 512])
        psQ_r = [p.ps(f"psQr{j}", [128, 512]) for j in range(3)]
        rot = [0]

        def nxq():
            b = psQ_r[rot[0] % 3]
            rot[0] += 1
            return b

        p.op(V, V.memset, epsa[:, 0:1], 1e-5)
        p.op(V, V.memset, pih[:], float(np.pi / 2))
        p.op(V, V.memset, mgc[:], MAGICF)
        p.op(V, V.memset, onescol_b[:], 1.0)

        def tile_gen(i):
            g, i4 = i // 4, i % 4
            tp = i % 2
            xt, xc, sqb, stats = xt_p[tp], xc_p[tp], sqb_p[tp], stats_p[tp]
            xlb, xfm, kf_s, vt_s = xlb_p[tp], xfm_p[tp], kf_p[tp], vt_p[tp]
            rmag, rint, fsb, fab, kp_s = (rmag_p[tp], rint_p[tp], fsb_p[tp],
                                          fab_p[tp], kp_p[tp])
            qf_g = qf_g_p[g % 2]
            p.dma(GP, xt[:], io["x2_row"](i))
            yield
            p.op(V, V.reduce_sum, stats[:, 0:1], xt[:], AX.X)
            yield
            p.op(V, V.tensor_scalar, stats[:, 1:2], stats[:, 0:1], 1.0 / D, None, ALU.mult)
            yield
            p.op(V, V.tensor_scalar, xc[:], xt[:], stats[:, 1:2], None, ALU.subtract)
            yield
            p.op(V, V.tensor_mul, sqb[:], xc[:], xc[:])
            yield
            p.op(V, V.reduce_sum, stats[:, 2:3], sqb[:], AX.X)
            yield
            p.op(SC, SC.activation, stats[:, 3:4], stats[:, 2:3], AF.Ln, epsa[:, 0:1], 1.0 / D)
            yield
            p.op(SC, SC.activation, stats[:, 4:5], stats[:, 3:4], AF.Exp, scale=-0.5)
            yield
            p.op(V, V.tensor_scalar, xlb[:], xc[:], stats[:, 4:5], None, ALU.mult)
            yield
            for c in range(8):
                p.dma(SY, xfm[:, c * 128:(c + 1) * 128],
                      xlb[:, c * 128:(c + 1) * 128], transpose=True)
                yield
            for (wT, bias_s, outf) in ((wqT, bq_s, None), (wkT, bk_s, kf_s)):
                for c in range(8):
                    ps = nxq()
                    p.mmgrp([(ps[:, 0:128],
                              wT[:, cin * 1024 + c * 128:cin * 1024 + (c + 1) * 128],
                              xfm[:, cin * 128:(cin + 1) * 128]) for cin in range(8)])
                    yield
                    if outf is None:
                        p.op(SC, SC.activation, qf_g[:, c * 512 + i4 * 128:c * 512 + (i4 + 1) * 128],
                             ps[:, 0:128], AF.Identity, bias_s[:, c:c + 1])
                    else:
                        p.op(SC, SC.activation, outf[:, c * 128:(c + 1) * 128],
                             ps[:, 0:128], AF.Identity, bias_s[:, c:c + 1])
                    yield
            for n0 in range(2):
                ps = nxq()
                p.mmgrp([(ps[:, 0:512],
                          xfm[:, cin * 128:(cin + 1) * 128],
                          wvT[:, cin * 1024 + n0 * 512:cin * 1024 + (n0 + 1) * 512])
                         for cin in range(8)])
                yield
                p.op(V, V.tensor_copy, vt_s[:, n0 * 512:(n0 + 1) * 512], ps[:, 0:512])
                yield
            if i4 == 3:
                for c in range(8):
                    p.dma(GP, io["qf_out"](c, g), qf_g[:, c * 512:(c + 1) * 512])
                    yield
            for h in range(16):
                b0 = (h % 2) * 64
                ps = nxq()
                p.mm(ps[:, 0:128], kf_s[b0:b0 + 64, (h // 2) * 128:(h // 2 + 1) * 128],
                     prj[b0:b0 + 64, :], start=True, stop=True)
                yield
                p.op(SC, SC.activation, rmag[:], ps[:, 0:128], AF.Identity, mgc[:, 0:1])
                yield
                p.op(V, V.tensor_scalar, rint[:], rmag[:], MAGICF, None, ALU.subtract)
                yield
                p.op(V, V.tensor_sub, fsb[:], ps[:, 0:128], rint[:])
                yield
                p.op(SC, SC.activation, kp_s[:, 128:256], fsb[:], AF.Sin, scale=TWO_PI)
                yield
                p.op(SC, SC.activation, fab[:], fsb[:], AF.Abs)
                yield
                p.op(SC, SC.activation, kp_s[:, 0:128], fab[:], AF.Sin, pih[:, 0:1], -TWO_PI)
                yield
                for mc in range(2):
                    r = h * 2 + mc
                    bank, co = r // 8, (r % 8) * 64
                    kv_start = (i == 0 and r % 8 == 0)
                    sk_start = (i == 0 and r == 0)
                    stop = (i == NT - 1)
                    p.mm(psKV[bank][:, co:co + 64], kp_s[:, mc * 128:(mc + 1) * 128],
                         vt_s[:, h * 64:(h + 1) * 64],
                         start=kv_start, stop=stop, skip_group_check=True)
                    p.mm(psSK[:, r:r + 1], kp_s[:, mc * 128:(mc + 1) * 128],
                         onescol_b[:], start=sk_start, stop=stop, skip_group_check=True)
                    yield

        pipeline(tile_gen, NT, 2)
        for bank in range(4):
            p.op(V, V.tensor_copy, kvout[:, bank * 512:(bank + 1) * 512], psKV[bank][:, 0:512])
        p.op(V, V.tensor_copy, skout[:], psSK[:, 0:32])
        for r in range(32):
            p.dma(GP, io["kv_out"](r), kvout[:, r * 64:(r + 1) * 64])
            p.dma(GP, io["sk_out"](r), skout[:, r:r + 1])



def emit_l4(nc, p, C, io):
    V, SC, GP, SY = nc.vector, nc.scalar, nc.gpsimd, nc.sync
    NT = T // 128
    with p.scope():
        be_s = p.sb("be_s", [1, 8 * D])
        p.dma(SY, be_s[:], io["be"])
        xhfm = p.sb("xhfm", [128, 8 * T], BF16)
        mg4 = p.sb("mg4", [128, 32 * 8])
        ctxA = p.scope()
        ctxA.__enter__()
        woT = p.sb("woT", [128, 8 * 1024], BF16)
        p.dma(GP, io["scrw"].ap(), io["wo1"])
        for c in range(8):
            p.dma(SY, woT[:, c * 1024:(c + 1) * 1024],
                  io["scrw"].ap()[0:1024, c * 128:(c + 1) * 128], transpose=True)
        bo_s = p.sb("bo_s", [1, D])
        p.dma(SY, bo_s[:], io["bo"])
        bv_s = p.sb("bv_s", [1, D])
        p.dma(SY, bv_s[:], io["bv"])
        rb_s = p.sb("rb_s", [1, 8])
        p.dma(SY, rb_s[:], io["rb4"])
        wr_s = p.sb("wr_s", [128, 8 * 8])
        p.dma(SY, wr_s[:].rearrange("p (c n) -> p c n", c=8),
              io["wr4"].rearrange("(c p) n -> p c n", p=128))
        wrh = p.sb("wrh", [128, 8 * 8], BF16)
        wrl = p.sb("wrl", [128, 8 * 8], BF16)
        p.op(V, V.tensor_copy, wrh[:], wr_s[:])
        p.op(V, V.tensor_sub, wrl[:], wr_s[:], wrh[:])
        prj = p.sb("prj", [128, 128], BF16)
        prf = p.sb("prf", [64, 128])
        p.dma(SY, prf[:], io["proj"])
        p.op(V, V.tensor_scalar, prj[0:64, :], prf[:], 1.0 / (8.0 * TWO_PI), None, ALU.mult)
        p.op(V, V.tensor_scalar, prj[64:128, :], prf[:], 1.0 / (8.0 * TWO_PI), None, ALU.mult)

        epsa = p.sb("epsa", [128, 1])
        pih = p.sb("pih", [128, 1])
        mgc = p.sb("mgc", [128, 1])
        kva = p.sb("kva", [128, 2048])
        kvb = p.sb("kvb", [128, 2048], BF16)
        ska = p.sb("ska", [128, 32])
        qf_g_p = [p.sb("qf_g", [128, 8 * 512], BF16) for _ in range(2)]
        rmag_p = [p.sb("rmag", [128, 128]) for _ in range(2)]
        rint_s = p.sb("rint", [128, 128])
        rint_p = [rint_s, rint_s]
        fsb_p = [p.sb("fsb", [128, 128]) for _ in range(2)]
        fab_s = p.sb("fab", [128, 128])
        fab_p = [fab_s, fab_s]
        qp_p = [p.sb("qp_s", [128, 256], BF16) for _ in range(2)]
        of_p = [p.sb("of_s", [128, 8 * 128], BF16) for _ in range(2)]
        xr_p = [p.sb("xr_t", [128, D]) for _ in range(2)]
        x3_p = [p.sb("x3_s", [128, D]) for _ in range(2)]
        xc_s = p.sb("xc", [128, D])
        xc_p = [xc_s, xc_s]
        sqb_s = p.sb("sqb", [128, D])
        sqb_p = [sqb_s, sqb_s]
        stats_p = [p.sb("stats", [128, 8]) for _ in range(2)]
        xlf_s = p.sb("xlf", [128, D])
        xlf_p = [xlf_s, xlf_s]
        xhi_p = [p.sb("xhi", [128, D], BF16) for _ in range(2)]
        xlo_p = [p.sb("xlo", [128, D], BF16) for _ in range(2)]
        xlofm_p = [p.sb("xlofm", [128, 8 * 128], BF16) for _ in range(2)]
        m1_p = [p.sb("m1", [128, 1]) for _ in range(2)]
        negm_p = [p.sb("negm", [128, 1]) for _ in range(2)]
        msk_p = [p.sb("msk", [128, 8]) for _ in range(2)]
        exb_p = [p.sb("exb", [128, 8]) for _ in range(2)]
        ssum_p = [p.sb("ssum", [128, 1]) for _ in range(2)]
        gat_p = [p.sb("gat", [128, 1]) for _ in range(2)]

        psR = [p.ps(f"psR4{j}", [128, 512]) for j in range(8)]
        rot = [0]

        def nxr():
            b = psR[rot[0] % 8]
            rot[0] += 1
            return b

        p.op(V, V.memset, epsa[:], 1e-5)
        p.op(V, V.memset, pih[:], float(np.pi / 2))
        p.op(V, V.memset, mgc[:], MAGICF)

        with p.scope():
            kvb_ = p.sb("kvb_", [128, 1024])
            skb = p.sb("skb", [128, 32])
            p.dma(SY, kva[:].rearrange("p (r n) -> p r n", r=32), io["kv0_re"])
            p.dma(SY, ska[:].rearrange("p (r n) -> p r n", r=32), io["sk0_re"],
                  allow_slow_non_contiguous=True)
            p.dma(SY, skb[:].rearrange("p (r n) -> p r n", r=32), io["sk1_re"],
                  allow_slow_non_contiguous=True)
            for half in range(2):
                p.dma(SY, kvb_[:].rearrange("p (r n) -> p r n", r=16),
                      io["kv1_re"][:, half * 16:(half + 1) * 16, :])
                p.op(V, V.tensor_add, kva[:, half * 1024:(half + 1) * 1024],
                     kva[:, half * 1024:(half + 1) * 1024], kvb_[:])
            p.op(V, V.tensor_add, ska[:], ska[:], skb[:])
        for r in range(32):
            h = r // 2
            p.mm(psR[0][:, 0:64], C["onesf"][:], bv_s[0:1, h * 64:(h + 1) * 64],
                 start=True, stop=True)
            p.op(V, V.scalar_tensor_tensor, kva[:, r * 64:(r + 1) * 64], psR[0][:, 0:64],
                 ska[:, r:r + 1], kva[:, r * 64:(r + 1) * 64], ALU.mult, ALU.add)
        p.op(V, V.tensor_copy, kvb[:], kva[:])

        def tile_gen(i):
            g, i4 = i // 4, i % 4
            tp = i % 2
            qf_g = qf_g_p[g % 2]
            rmag, rint, fsb, fab, qp_s = (rmag_p[tp], rint_p[tp], fsb_p[tp],
                                          fab_p[tp], qp_p[tp])
            of_s, xr_t, x3_s, xc, sqb = (of_p[tp], xr_p[tp], x3_p[tp],
                                         xc_p[tp], sqb_p[tp])
            stats, xlf, xhi, xlo, xlofm = (stats_p[tp], xlf_p[tp], xhi_p[tp],
                                           xlo_p[tp], xlofm_p[tp])
            m1, negm, msk = m1_p[tp], negm_p[tp], msk_p[tp]
            exb, ssum, gat = exb_p[tp], ssum_p[tp], gat_p[tp]
            if i4 == 0:
                for c in range(8):
                    p.dma(GP, qf_g[:, c * 512:(c + 1) * 512], io["qf_in"](c, g))
                    yield
            p.dma(GP, xr_t[:], io["x2_row"](i))
            yield
            for h in range(16):
                b0 = (h % 2) * 64
                psq = nxr()
                p.mm(psq[:, 0:128], prj[b0:b0 + 64, :],
                     qf_g[b0:b0 + 64, (h // 2) * 512 + i4 * 128:
                          (h // 2) * 512 + (i4 + 1) * 128], start=True, stop=True)
                yield
                p.op(SC, SC.activation, rmag[:], psq[:, 0:128], AF.Identity, mgc[:, 0:1])
                yield
                p.op(V, V.tensor_scalar, rint[:], rmag[:], MAGICF, None, ALU.subtract)
                p.op(V, V.tensor_sub, fsb[:], psq[:, 0:128], rint[:])
                yield
                p.op(SC, SC.activation, qp_s[:, 128:256], fsb[:], AF.Sin, scale=TWO_PI)
                yield
                p.op(SC, SC.activation, fab[:], fsb[:], AF.Abs)
                p.op(SC, SC.activation, qp_s[:, 0:128], fab[:], AF.Sin, pih[:, 0:1], -TWO_PI)
                yield
                pso = nxr()
                p.mmgrp([(pso[0:64, 0:128], kvb[:, (h * 2 + mc) * 64:(h * 2 + mc + 1) * 64],
                          qp_s[:, mc * 128:(mc + 1) * 128]) for mc in range(2)])
                yield
                p.op(V, V.tensor_scalar, of_s[b0:b0 + 64, (h // 2) * 128:(h // 2 + 1) * 128],
                     pso[0:64, 0:128], 1.0 / 128.0, None, ALU.mult)
                yield
            for n0 in range(2):
                psy = nxr()
                calls = [(psy[:, 0:512],
                          of_s[:, c * 128:(c + 1) * 128],
                          woT[:, c * 1024 + n0 * 512:c * 1024 + (n0 + 1) * 512])
                         for c in range(8)]
                calls.append((psy[:, 0:512], C["onesf"][:],
                              bo_s[0:1, n0 * 512:(n0 + 1) * 512]))
                p.mmgrp(calls)
                yield
                p.op(V, V.tensor_add, x3_s[:, n0 * 512:(n0 + 1) * 512],
                     psy[:, 0:512], xr_t[:, n0 * 512:(n0 + 1) * 512])
                yield
            p.dma(GP, io["xo_row"](i), x3_s[:])
            yield
            p.op(V, V.reduce_sum, stats[:, 0:1], x3_s[:], AX.X)
            yield
            p.op(V, V.tensor_scalar, stats[:, 1:2], stats[:, 0:1], 1.0 / D, None, ALU.mult)
            yield
            p.op(V, V.tensor_scalar, xc[:], x3_s[:], stats[:, 1:2], None, ALU.subtract)
            p.op(V, V.tensor_mul, sqb[:], xc[:], xc[:])
            p.op(V, V.reduce_sum, stats[:, 2:3], sqb[:], AX.X)
            p.op(SC, SC.activation, stats[:, 3:4], stats[:, 2:3], AF.Ln, epsa[:, 0:1], 1.0 / D)
            p.op(SC, SC.activation, stats[:, 4:5], stats[:, 3:4], AF.Exp, scale=-0.5)
            p.op(V, V.tensor_scalar, xlf[:], xc[:], stats[:, 4:5], None, ALU.mult)
            p.op(V, V.tensor_copy, xhi[:], xlf[:])
            p.op(V, V.tensor_sub, xlo[:], xlf[:], xhi[:])
            yield
            for c in range(8):
                p.dma(SY, xhfm[:, c * T + i * 128:c * T + (i + 1) * 128],
                      xhi[:, c * 128:(c + 1) * 128], transpose=True)
                yield
            for c in range(8):
                p.dma(SY, xlofm[:, c * 128:(c + 1) * 128],
                      xlo[:, c * 128:(c + 1) * 128], transpose=True)
                yield
            psf = nxr()
            calls = []
            for c in range(8):
                xh = xhfm[:, c * T + i * 128:c * T + (i + 1) * 128]
                xl = xlofm[:, c * 128:(c + 1) * 128]
                calls.append((psf[:, 0:8], xh, wrh[:, c * 8:(c + 1) * 8]))
                calls.append((psf[:, 0:8], xh, wrl[:, c * 8:(c + 1) * 8]))
                calls.append((psf[:, 0:8], xl, wrh[:, c * 8:(c + 1) * 8]))
            calls.append((psf[:, 0:8], C["onesf"][:], rb_s[0:1, 0:8]))
            p.mmgrp(calls)
            yield
            p.op(V, V.reduce_max, m1[:], psf[:, 0:8], AX.X)
            yield
            p.op(V, V.tensor_tensor, msk[:], psf[:, 0:8],
                 m1[:].to_broadcast([128, 8]), ALU.is_equal)
            yield
            p.op(V, V.tensor_scalar, negm[:], m1[:], -1.0, None, ALU.mult)
            yield
            p.op(SC, SC.activation, exb[:], psf[:, 0:8], AF.Exp, negm[:, 0:1])
            yield
            p.op(V, V.reduce_sum, ssum[:], exb[:], AX.X)
            yield
            p.op(V, V.reciprocal, gat[:], ssum[:])
            yield
            p.op(V, V.tensor_scalar, mg4[:, i * 8:(i + 1) * 8], msk[:], gat[:, 0:1], None, ALU.mult)
            yield

        pipeline(tile_gen, NT, 2)

        ctxA.__exit__(None, None, None)
        p.floor_fence()
        with p.scope():
            web = [p.sb("web", [128, 8 * 1024], BF16) for _ in range(2)]
            psYq = [p.ps("psYq", [128, 1024]) for _ in range(2)]
            ysbq = [p.sb("ysbq", [128, 1024]) for _ in range(2)]

            def ldwe(e):
                p.dma(GP, web[e % 2][:].rearrange("p (c n) -> p c n", c=8),
                      io["we_e"](e).rearrange("(c p) n -> p c n", p=128), bulk=True)
            ldwe(0)
            for e in range(8):
                if e + 1 < 8:
                    ldwe(e + 1)
                we_s = web[e % 2]
                for i in range(NT):
                    q = i % 2
                    psYe, ysbe = psYq[q], ysbq[q]
                    for n0 in range(2):
                        calls = [(psYe[:, n0 * 512:(n0 + 1) * 512],
                                  xhfm[:, cin * T + i * 128:cin * T + (i + 1) * 128],
                                  we_s[:, cin * 1024 + n0 * 512:cin * 1024 + (n0 + 1) * 512])
                                 for cin in range(8)]
                        calls.append((psYe[:, n0 * 512:(n0 + 1) * 512], C["onesf"][:],
                                      be_s[0:1, e * 1024 + n0 * 512:e * 1024 + (n0 + 1) * 512]))
                        p.mmgrp(calls)
                    if q == 0:
                        p.op(V, V.tensor_scalar, ysbe[:], psYe[:],
                             mg4[:, i * 8 + e:i * 8 + e + 1], None, ALU.mult)
                    else:
                        p.op(SC, SC.activation, ysbe[:], psYe[:], AF.Identity,
                             scale=mg4[:, i * 8 + e:i * 8 + e + 1])
                    p.dma(GP, io["xo_row"](i), ysbe[:], accum_op=ALU.add)



def build_fused():
    nc = bass.Bass("TRN2", target_bir_lowering=False, debug=False)
    ein = lambda n, s, dt=F32: nc.dram_tensor(n, s, dt, kind="ExternalInput").ap()
    x = ein("x", [LSEQ, D])
    wq = ein("wq", [D, 512]); wk = ein("wk", [D, 512]); wv = ein("wv", [D, 512])
    cq = ein("cq", [512, 4]); ck = ein("ck", [512, 4]); cv = ein("cv", [512, 4])
    wbg = ein("wbg", [D, 4]); wmx = ein("wmx", [D, 2]); cmx = ein("cmx", [2, 4])
    mbias = ein("mbias", [2, 1]); rmsw = ein("rmsw", [128, 2])
    wo = ein("wo", [D, D]); wr = ein("wr", [D, 8])
    w1 = ein("w1", [8, D, D]); w2 = ein("w2", [8, D, D])
    wq1 = ein("wq1", [D, D]); wk1 = ein("wk1", [D, D]); wv1 = ein("wv1", [D, D])
    bq = ein("bq", [128, 8]); bk = ein("bk", [128, 8]); proj = ein("proj", [64, 128])
    bv = ein("bv", [1, D]); wo1 = ein("wo1", [D, D]); bo = ein("bo", [1, D])
    wr4 = ein("wr4", [D, 8]); rb4 = ein("rb4", [1, 8])
    we = ein("we", [8, D, D]); be = ein("be", [1, 8 * D])
    slot = ein("slot", [1, 1], I32); nonce = ein("nonce", [1, 1], I32)
    xo = nc.dram_tensor("xo", [T, D], F32, kind="ExternalOutput").ap()

    om_sh = nc.dram_tensor("om_sh", [2 * 4 * 16 * 128, 512], BF16, kind="Internal", addr_space="Shared")
    kv_sh = nc.dram_tensor("kv_sh", [2 * 4096, 64], F32, kind="Internal", addr_space="Shared")
    sk_sh = nc.dram_tensor("sk_sh", [2 * 4096, 1], F32, kind="Internal", addr_space="Shared")
    fl_sh = nc.dram_tensor("fl_sh", [2 * 2, 1], I32, kind="Internal", addr_space="Shared")
    x2_i = nc.dram_tensor("x2_i", [T, D], F32, kind="Internal")
    qf_i = nc.dram_tensor("qf_i", [D, T], BF16, kind="Internal")
    scrq = nc.dram_tensor("scrq", [D, D], BF16, kind="Internal")
    scrk = nc.dram_tensor("scrk", [D, D], BF16, kind="Internal")
    scrv = nc.dram_tensor("scrv", [D, D], BF16, kind="Internal")
    scrw = nc.dram_tensor("scrw", [D, D], BF16, kind="Internal")

    import os
    with ExitStack() as ctx:
        p = (Pr if os.environ.get("SERIAL_SCHED") else Pr2)(nc, ctx)
        C = _consts(nc, p)
        V, SC, GP, SY = nc.vector, nc.scalar, nc.gpsimd, nc.sync

        slt = p.sb("slt", [1, 2], I32)
        flt = p.sb("flt", [1, 1], I32)
        psem = ctx.enter_context(nc.semaphore("psem"))
        p.dma(SY, slt[0:1, 0:1], slot)
        p.dma(SY, slt[0:1, 1:2], nonce)
        p._w(GP)
        p._w(SY)
        p._w(SC)
        rs_r = GP.alloc_register("rs_r")
        rn_r = GP.alloc_register("rn_r")
        rf_r = GP.alloc_register("rf_r")
        rc_r = GP.alloc_register("rc_r")
        cnd_r = GP.alloc_register("cnd_r")
        GP.reg_load(rs_r, slt[0:1, 0:1])
        GP.reg_load(rn_r, slt[0:1, 1:2])
        GP.reg_mov(rc_r, 0)
        p.slot_reg = rs_r
        rs_sy = SY.alloc_register("rs_sy")
        SY.reg_load(rs_sy, slt[0:1, 0:1])
        rs_sc = SC.alloc_register("rs_sc")
        SC.reg_load(rs_sc, slt[0:1, 0:1])
        p.slot_regs = {"gp": rs_r, "sync": rs_sy, "sc": rs_sc}

        def poll_partner(col, partner):
            GP.reg_mov(cnd_r, 1)
            with GP.While(cnd_r):
                GP.dma_start(out=flt[0:1, 0:1],
                             in_=fl_sh.ap()[partner * 2 + col:partner * 2 + col + 1, :]
                             ).then_inc(psem, 16)
                GP.reg_add(rc_r, rc_r, 16)
                GP.wait_ge(psem, rc_r)
                GP.reg_load(rf_r, flt[0:1, 0:1])
                GP.reg_alu(cnd_r, rf_r, rn_r, ALU.not_equal)

        def barrier(col):
            p._w(GP)
            p.dma(GP, (fl_sh.ap()[col:col + 1, :], fl_sh.ap()[2 + col:3 + col, :]), nonce)
            p._w(GP)
            with GP.If_eq(rs_r, 0):
                poll_partner(col, 1)
            with GP.Else():
                poll_partner(col, 0)
            p.set_fence(GP)

        io1 = {
            "x": x, "wq": wq, "wk": wk, "wv": wv, "cq": cq, "ck": ck, "cv": cv,
            "wbg": wbg, "wmx": wmx, "cmx": cmx, "mbias": mbias, "rmsw": rmsw,
            "om_out": lambda c4, t0: tuple(
                om_sh.ap()[((sl * 4 + c4) * 16 + t0 // 512) * 128:
                           ((sl * 4 + c4) * 16 + t0 // 512) * 128 + 128, :]
                for sl in (0, 1)),
        }
        import os
        smask = os.environ.get("SERMASK", "0000")
        p.force_serial = smask[0] == "1"
        emit_l1(nc, p, C, io1)
        p.force_serial = False
        barrier(0)

        io2 = {
            "om_in": lambda c, g: tuple(
                om_sh.ap()[(c // 4) * 8192 + (c % 4) * 2048 + sl * 1024 + g * 128:
                           (c // 4) * 8192 + (c % 4) * 2048 + sl * 1024 + g * 128 + 128, :]
                for sl in (0, 1)),
            "xr_row": lambda i: tuple(
                x[sl * T + i * 128:sl * T + (i + 1) * 128, :] for sl in (0, 1)),
            "x2_row": lambda i: x2_i.ap()[i * 128:(i + 1) * 128, :],
            "wo": wo, "wr": wr,
            "w1e": lambda e: w1[e], "w2e": lambda e: w2[e],
        }
        p.force_serial = smask[1] == "1"
        emit_l2(nc, p, C, io2)
        p.force_serial = False
        p.floor_fence()

        io3 = {
            "x2_row": lambda i: x2_i.ap()[i * 128:(i + 1) * 128, :],
            "wq1": wq1, "wk1": wk1, "wv1": wv1, "bq": bq, "bk": bk, "proj": proj,
            "scrq": scrq, "scrk": scrk, "scrv": scrv,
            "kv_out": lambda r: tuple(
                kv_sh.ap()[sl * 4096 + r * 128:sl * 4096 + (r + 1) * 128, :] for sl in (0, 1)),
            "sk_out": lambda r: tuple(
                sk_sh.ap()[sl * 4096 + r * 128:sl * 4096 + (r + 1) * 128, :] for sl in (0, 1)),
            "qf_out": lambda c, g: qf_i.ap()[c * 128:(c + 1) * 128, g * 512:(g + 1) * 512],
        }
        p.force_serial = smask[2] == "1"
        emit_l3(nc, p, C, io3)
        p.force_serial = False
        barrier(1)

        io4 = {
            "kv0_re": kv_sh.ap()[0:4096, :].rearrange("(r p) n -> p r n", p=128),
            "kv1_re": kv_sh.ap()[4096:8192, :].rearrange("(r p) n -> p r n", p=128),
            "sk0_re": sk_sh.ap()[0:4096, :].rearrange("(r p) n -> p r n", p=128),
            "sk1_re": sk_sh.ap()[4096:8192, :].rearrange("(r p) n -> p r n", p=128),
            "qf_in": lambda c, g: qf_i.ap()[c * 128:(c + 1) * 128, g * 512:(g + 1) * 512],
            "x2_row": lambda i: x2_i.ap()[i * 128:(i + 1) * 128, :],
            "bv": bv, "wo1": wo1, "bo": bo, "wr4": wr4, "rb4": rb4, "be": be,
            "proj": proj, "scrw": scrw,
            "we_e": lambda e: we[e],
            "xo_row": lambda i: xo[i * 128:(i + 1) * 128, :],
        }
        p.force_serial = smask[3] == "1"
        emit_l4(nc, p, C, io4)
        p.force_serial = False
        p.fin(GP)
    return nc


def kernel(**inputs):
    I = {k: np.asarray(v) for k, v in inputs.items()}
    x0 = np.ascontiguousarray(I["L0_attn_hidden_states"].astype(np.float32))
    cores = list(range(8))
    nonce = np.int32((time.time_ns() // 1000) & 0x7FFFFFFF)

    nc = build_fused()
    maps = []
    for c in cores:
        b, sl = c // 2, c % 2
        hsl = slice(sl * 512, (sl + 1) * 512)
        hh = slice(2 * sl, 2 * sl + 2)
        maps.append({
            "x": x0[b],
            "wq": np.ascontiguousarray(I["L0_attn_Wq"][:, hsl]),
            "wk": np.ascontiguousarray(I["L0_attn_Wk"][:, hsl]),
            "wv": np.ascontiguousarray(I["L0_attn_Wv"][:, hsl]),
            "cq": np.ascontiguousarray(I["L0_attn_conv_q"][hsl]),
            "ck": np.ascontiguousarray(I["L0_attn_conv_k"][hsl]),
            "cv": np.ascontiguousarray(I["L0_attn_conv_v"][hsl]),
            "wbg": np.ascontiguousarray(np.concatenate(
                [I["L0_attn_Wb"][:, hh], I["L0_attn_Wdec"][:, hh]], axis=1)),
            "wmx": np.ascontiguousarray(I["L0_attn_Wmix"][:, hh]),
            "cmx": np.ascontiguousarray(I["L0_attn_conv_mix"][hh]),
            "mbias": np.ascontiguousarray(I["L0_attn_mix_bias"][hh][:, None]),
            "rmsw": np.ascontiguousarray(I["L0_attn_rms_w"].reshape(2, 128).T),
            "wo": I["L0_attn_Wo"].astype(np.float32),
            "wr": I["L0_ffn_w_router"].astype(np.float32),
            "w1": I["L0_ffn_w1"].astype(np.float32),
            "w2": I["L0_ffn_w2"].astype(np.float32),
            "wq1": I["L1_attn_Wq"].astype(np.float32),
            "wk1": I["L1_attn_Wk"].astype(np.float32),
            "wv1": I["L1_attn_Wv"].astype(np.float32),
            "bq": np.ascontiguousarray(I["L1_attn_bq"].astype(np.float32).reshape(8, 128).T),
            "bk": np.ascontiguousarray(I["L1_attn_bk"].astype(np.float32).reshape(8, 128).T),
            "proj": I["L1_attn_proj_mat"].astype(np.float32),
            "bv": np.ascontiguousarray(I["L1_attn_bv"].astype(np.float32).reshape(1, D)),
            "wo1": I["L1_attn_Wo"].astype(np.float32),
            "bo": np.ascontiguousarray(I["L1_attn_bo"].astype(np.float32).reshape(1, D)),
            "wr4": I["L1_ffn_router_w"].astype(np.float32),
            "rb4": np.ascontiguousarray(I["L1_ffn_router_b"].astype(np.float32).reshape(1, 8)),
            "we": I["L1_ffn_expert_w"].astype(np.float32),
            "be": np.ascontiguousarray(I["L1_ffn_expert_b"].astype(np.float32).reshape(1, 8 * D)),
            "slot": np.array([[sl]], np.int32),
            "nonce": np.array([[nonce]], np.int32),
        })
    res = run_bass_kernel_spmd(nc, maps, core_ids=cores)

    out = np.empty((4, LSEQ, D), dtype=np.float32)
    for c in cores:
        b, sl = c // 2, c % 2
        out[b, sl * T:(sl + 1) * T] = res.results[c]["xo"]
    return out
